# version 13
# baseline (speedup 1.0000x reference)
import contextlib
import numpy as np
import concourse.bass as bass
import concourse.mybir as mybir
from concourse.bass_utils import run_bass_kernel_spmd

F32 = mybir.dt.float32
BF16 = mybir.dt.bfloat16
AF = mybir.ActivationFunctionType
ALU = mybir.AluOpType
AX = mybir.AxisListType

H = 2048
B = 4
T = 2048
NT = 1024
W = NT + 2
FC = 16
D4 = 8192
DC4 = 64
EPS = 1e-5
E_RT = 16
E_SH = 2
DCE = 8
TB = 512
IH = 1.0 / H
FCW = FC * W

ENGINES = ("tensor", "vector", "scalar", "sync", "gpsimd")


class _Sem:
    __slots__ = ("name", "cnt")

    def __init__(self, name):
        self.name = name
        self.cnt = 0


class Sched:

    def __init__(self):
        self.steps = {e: [] for e in ENGINES}
        self.sems = []

    def sem(self, name):
        s = _Sem(name)
        self.sems.append(s)
        return s

    def emit(self, eng, fn, waits=(), inc=None, inc_n=1):
        w = []
        for s, v in waits:
            assert v <= s.cnt, f"wait on future: {s.name} {v} > {s.cnt} ({eng})"
            if v > 0:
                w.append((s, v))
        self.steps[eng].append((tuple(w), fn, inc, inc_n))
        if inc is not None:
            inc.cnt += inc_n
            return inc.cnt
        return None


def _tile_kxm(wT):
    K, M = wT.shape
    KC, MB = K // 128, M // 128
    a = wT.reshape(KC, 128, MB, 128).transpose(2, 1, 0, 3).reshape(MB, 128, KC * 128)
    return np.ascontiguousarray(a)


def build_nc():
    nc = bass.Bass("TRN2", target_bir_lowering=False)

    dt = {}
    dt["x0"] = nc.dram_tensor("x0", [H, W], F32, kind="ExternalInput")
    dt["crow"] = nc.dram_tensor("crow", [16, 128], F32, kind="ExternalInput")
    dt["hmask"] = nc.dram_tensor("hmask", [128, 1], F32, kind="ExternalInput")
    dt["vecs"] = nc.dram_tensor("vecs", [10, H], F32, kind="ExternalInput")
    dt["wmod"] = nc.dram_tensor("wmod", [96, 128, FC * 128], F32, kind="ExternalInput")
    for nm in ("wr", "wk", "wv", "wo", "wcr"):
        dt[nm] = nc.dram_tensor(nm, [16, 128, FC * 128], F32, kind="ExternalInput")
    dt["wkey"] = nc.dram_tensor("wkey", [64, 128, FC * 128], F32, kind="ExternalInput")
    dt["wval"] = nc.dram_tensor("wval", [16, 128, DC4 * 128], F32, kind="ExternalInput")
    dt["wrouter"] = nc.dram_tensor("wrouter", [128, 256], F32, kind="ExternalInput")
    dt["wshu"] = nc.dram_tensor("wshu", [E_SH * DCE, 128, FC * 128], F32, kind="ExternalInput")
    dt["wshd"] = nc.dram_tensor("wshd", [E_SH * FC, 128, DCE * 128], F32, kind="ExternalInput")
    dt["wrtu"] = nc.dram_tensor("wrtu", [E_RT * DCE, 128, FC * 128], F32, kind="ExternalInput")
    dt["wrtd"] = nc.dram_tensor("wrtd", [E_RT * FC, 128, DCE * 128], F32, kind="ExternalInput")
    dt["out"] = nc.dram_tensor("out", [H, NT], F32, kind="ExternalOutput")
    dt["xasp"] = nc.dram_tensor("xasp", [H, NT], F32, kind="Internal")
    dt["xbsp"] = nc.dram_tensor("xbsp", [H, TB], F32, kind="Internal")
    dt["x1sp"] = nc.dram_tensor("x1sp", [H, NT], F32, kind="Internal")

    S = Sched()
    sWDs = [S.sem(f"sWD{i}") for i in range(3)]
    sWR = S.sem("sWR")
    sAD = S.sem("sAD")
    sPE = S.sem("sPE")
    sDV = S.sem("sDV")
    sAC = S.sem("sAC")
    sGP = S.sem("sGP")

    ctx = contextlib.ExitStack()
    sb = lambda name, shape, dtyp: ctx.enter_context(nc.sbuf_tensor(name, shape, dtyp))
    psf = [ctx.enter_context(nc.psum_tensor(f"p{i}", [128, 512], F32)) for i in range(7)]
    p7b = ctx.enter_context(nc.psum_tensor("p7b", [128, 1024], BF16))

    B1f = sb("B1f", [128, FCW], F32)
    B2f = sb("B2f", [128, FCW], F32)
    B3b = sb("B3b", [128, FCW], BF16)
    B1bt = nc.alloc_sbuf_tensor_at("B1B", [128, 2 * FCW], BF16,
                                   offset=nc.lookup_mloc(B1f).addr)
    B2bt = nc.alloc_sbuf_tensor_at("B2B", [128, 2 * FCW], BF16,
                                   offset=nc.lookup_mloc(B2f).addr)
    B1b = B1bt[:, :]
    B2b = B2bt[:, :]
    S0 = sb("S0", [128, W + 1], F32)
    S1 = sb("S1", [128, W], F32)
    BCB = sb("BCB", [128, 2 * W], BF16)
    S0B = sb("S0B", [128, W + 1], BF16)
    DT = sb("DT", [128, W], BF16)
    SMALL = sb("SMALL", [16, W], F32)
    STATROW = SMALL[0:1, 0:1026]
    PT = sb("PT", [128, 128], F32)
    AUX = sb("AUX", [128, 64], F32)
    PTX = AUX[:, 48:64]
    CONSTV = sb("CONSTV", [128, 224], F32)
    MODV = sb("MODV", [128, 96], F32)
    WQ = sb("WQ", [128, 3 * 2048], BF16)
    WKVT = sb("WKVT", [128, 1026], F32)
    WRT = sb("WRT", [128, 256], BF16)
    GATEB = sb("GATEB", [128, NT], BF16)
    IDF = sb("IDF", [128, 128], F32)
    IDB = sb("IDB", [128, 128], BF16)
    ONESF = sb("ONESF", [1, 128], F32)
    ONESB = sb("ONESB", [128, 1], BF16)
    ONESFB = sb("ONESFB", [128, 1], F32)
    SELW = sb("SELW", [16, 128], F32)
    R16 = sb("R16", [16, 128], F32)
    EPSC = sb("EPSC", [1, 1], F32)
    NEGC = sb("NEGC", [128, 1], F32)
    SSB = sb("SSB", [128, 16], BF16)
    SSTGT = sb("SSTG", [16, 128], BF16)
    HMSK = sb("HMSK", [128, 1], F32)

    CSLOT = {"mk": 0, "mv": 1, "mr": 2, "cmk": 3, "cmr": 4, "g1": 5, "b1": 6,
             "g2": 7, "b2": 8, "tmf": 9, "ca1": 10, "cb1": 11, "gac": 12,
             "gbc": 13}

    def cv(slot, fc):
        j = CSLOT[slot]
        return CONSTV[:, 16 * j + fc: 16 * j + fc + 1]

    def cvall(slot):
        j = CSLOT[slot]
        return CONSTV[:, 16 * j: 16 * j + 16]

    wq_idx = [0]
    wq_slot_pe = [0, 0, 0]
    bank_state = [None] * 7
    aux_banks = [5, 6]
    aux_i = [0]

    def wq_dma(src_ap):
        q = wq_idx[0]
        wq_idx[0] += 1
        slot = q % 3
        dst = WQ[:, slot * 2048: slot * 2048 + src_ap.shape[-1]]
        waits = []
        if wq_slot_pe[slot] > 0:
            waits.append((sPE, wq_slot_pe[slot]))
        S.emit("gpsimd", lambda g, d=dst, s=src_ap: g.dma_start(d, s),
               waits=waits, inc=sWDs[slot], inc_n=16)
        return dst, (sWDs[slot], sWDs[slot].cnt), slot

    sAD2 = S.sem("sAD2")

    def scal_dma(fn, waits=()):
        S.emit("scalar", fn, waits=waits, inc=sAD2, inc_n=16)
        v = sAD2.cnt
        S.emit("scalar", lambda sc: sc.nop(), waits=[(sAD2, v)])
        return v

    def sync_dma(fn, waits=()):
        S.emit("sync", fn, waits=waits, inc=sAD, inc_n=16)
        v = sAD.cnt
        S.emit("sync", lambda s: s.nop(), waits=[(sAD, v)])
        return v

    def bank_wait(bank):
        st = bank_state[bank]
        return [st] if st is not None else []

    def bank_done(bank, sem, val):
        bank_state[bank] = (sem, val)

    def aux_next():
        b = aux_banks[aux_i[0] % len(aux_banks)]
        aux_i[0] += 1
        return b

    v_hmsk = sync_dma(lambda s: s.dma_start(HMSK[:, :], dt["hmask"][:, :]))
    v_vecs = sync_dma(lambda s: s.dma_start(
        B2f[0:16, 0:1280].rearrange("p (j c) -> p j c", c=128),
        dt["vecs"][:, :].rearrange("j (p c) -> p j c", p=16)))
    v_crow = sync_dma(lambda s: s.dma_start(B2f[0:16, 1280:1408],
                                            dt["crow"][:, :]))
    v_x0 = sync_dma(lambda s: s.dma_start(
        B1f[:, :].rearrange("p (f t) -> p f t", t=W),
        dt["x0"][:, :].rearrange("(f p) t -> p f t", p=128)))
    S.emit("gpsimd", lambda g: g.dma_start(WRT[:, :], dt["wrouter"][:, :]),
           inc=sWR, inc_n=16)
    v_wrt = sWR.cnt

    S.emit("gpsimd", lambda g: g.iota(B2f[:, 1408:1536], [[1, 128]],
                                      channel_multiplier=-1,
                                      allow_small_or_imprecise_dtypes=True),
           waits=[(sAD, v_vecs)], inc=sGP)
    S.emit("gpsimd", lambda g: g.iota(R16[:, :], [[0, 128]],
                                      channel_multiplier=1,
                                      allow_small_or_imprecise_dtypes=True),
           inc=sGP)
    v_iota = sGP.cnt
    S.emit("vector", lambda v: v.tensor_scalar(IDF[:, :], B2f[:, 1408:1536],
                                               0.0, None, op0=ALU.is_equal),
           waits=[(sGP, v_iota)], inc=sDV)
    S.emit("vector", lambda v: v.tensor_copy(IDB[:, :], IDF[:, :]), inc=sDV)
    S.emit("vector", lambda v: v.memset(ONESF[:, :], 1.0), inc=sDV)
    S.emit("vector", lambda v: v.memset(ONESB[:, :], 1.0), inc=sDV)
    S.emit("vector", lambda v: v.memset(ONESFB[:, :], 1.0), inc=sDV)
    S.emit("vector", lambda v: v.memset(S0[:, 0:1], 0.0), inc=sDV)
    S.emit("vector", lambda v: v.memset(S0B[:, 0:1], 0.0), inc=sDV)
    S.emit("vector", lambda v: v.memset(EPSC[:, :], EPS), inc=sDV)
    S.emit("vector", lambda v: v.memset(NEGC[:, :], -1.0e38), inc=sDV)
    v_consts = sDV.cnt

    SSTG = SSTGT[0:16, :]
    S.emit("scalar", lambda sc: sc.activation(SSTG, B2f[0:16, 1280:1408],
                                              AF.Silu, bias=0.0, scale=1.0),
           waits=[(sAD, v_crow)], inc=sAC)
    v_sstg = sAC.cnt
    pe_v = S.emit("tensor", lambda t: t.transpose(p7b[:, 0:16], SSTG, IDB[0:16, 0:16]),
                  waits=[(sAC, v_sstg), (sDV, v_consts)], inc=sPE)
    v_ssb = S.emit("scalar", lambda sc: sc.copy(SSB[:, :], p7b[:, 0:16]),
                   waits=[(sPE, pe_v)], inc=sAC)

    VEC_ORDER = ["mk", "mv", "mr", "cmk", "cmr", "g1", "b1", "g2", "b2", "tmf"]
    for j, nm in enumerate(VEC_ORDER):
        bk = aux_next()
        wl = [(sAD, v_vecs), (sDV, v_consts)] + bank_wait(bk)
        pe_v = S.emit("tensor",
                      lambda t, j_=j, b=bk: t.transpose(
                          psf[b][0:128, 0:16], B2f[0:16, j_ * 128:(j_ + 1) * 128],
                          IDF[0:16, 0:16]),
                      waits=wl, inc=sPE)
        ac_v = S.emit("scalar",
                      lambda sc, nm_=nm, b=bk: sc.copy(cvall(nm_), psf[b][0:128, 0:16]),
                      waits=[(sPE, pe_v)], inc=sAC)
        bank_done(bk, sAC, ac_v)

    sWM = [S.sem("sWM0"), S.sem("sWM1")]
    wm_half_pe = [0, 0]
    lna_sum_pe = [0, 0, 0]
    lna_bank_first = [True, True, True]

    def _lna_sums(fc):
        for nb in range(3):
            nbw = [342, 342, 342][nb]
            w0 = nb * 342
            bk = 4 + nb
            waits = [(sAD, v_x0)]
            if lna_bank_first[nb]:
                waits += bank_wait(bk)
            r = S.emit(
                "tensor",
                lambda t, b=bk, st=lna_bank_first[nb], sp=(fc == FC - 1),
                f=fc, w0_=w0, n_=nbw:
                t.matmul(psf[b][0:1, 0:n_], ONESFB[:, :],
                         B1f[:, f * W + w0_: f * W + w0_ + n_],
                         start=st, stop=sp),
                waits=waits, inc=(sPE if fc == FC - 1 else None))
            lna_bank_first[nb] = False
            if r is not None:
                lna_sum_pe[nb] = r

    for g in range(24):
        half = g % 2
        hbase = half * 8192
        wl_d = [(sPE, wm_half_pe[half])] if wm_half_pe[half] else []
        S.emit("gpsimd", lambda gg, h=hbase, g_=g: gg.dma_start(
            B3b[:, h:h + 8192].rearrange("p (q c) -> p q c", c=2048),
            dt["wmod"][g_ * 4:(g_ + 1) * 4, :, :].transpose([1, 0, 2])),
            waits=wl_d, inc=sWM[half], inc_n=16)
        v_rdy = sWM[half].cnt
        bk = g % 2
        wl = [(sWM[half], v_rdy), (sAC, v_ssb)] + bank_wait(bk)
        pe_v = None
        for k in range(FC):
            r = S.emit("tensor",
                       lambda t, h=hbase, k_=k, b=bk, st=(k == 0),
                       sp=(k == FC - 1):
                       t.matmul(psf[b][0:1, 0:512],
                                SSB[:, k_:k_ + 1],
                                B3b[:, h:h + 8192]
                                .rearrange("p (q c) -> p q c", c=2048)
                                [:, :, k_ * 128:(k_ + 1) * 128],
                                start=st, stop=sp),
                       waits=(wl if k == 0 else []),
                       inc=(sPE if k == FC - 1 else None))
            pe_v = r if r is not None else pe_v
        wm_half_pe[half] = pe_v
        ac_row = S.emit("scalar", lambda sc, b=bk: sc.copy(
            S1[0:1, 0:512], psf[b][0:1, 0:512]),
            waits=[(sPE, pe_v)], inc=sAC)
        bank_done(bk, sAC, ac_row)
        for tq in range(4):
            Mb = g * 4 + tq
            bt = 2 + tq % 2
            pe_t = S.emit("tensor", lambda t, b=bt, c=tq * 128: t.transpose(
                psf[b][:, 0:1], S1[0:1, c:c + 128], IDF[0:1, 0:1]),
                waits=[(sAC, ac_row)] + bank_wait(bt), inc=sPE)
            ac_v = S.emit("scalar", lambda sc, b=bt, m=Mb: sc.copy(
                MODV[:, m:m + 1], psf[b][:, 0:1]),
                waits=[(sPE, pe_t)], inc=sAC)
            bank_done(bt, sAC, ac_v)
        if 2 <= g <= 17:
            _lna_sums(g - 2)
    for bk in (4, 5, 6):
        bank_done(bk, sPE, lna_sum_pe[bk - 4])
    v_mod = sAC.cnt

    S.emit("vector", lambda v: v.tensor_scalar(cvall("ca1"), MODV[:, 16:32],
                                               1.0, None, op0=ALU.add),
           waits=[(sAC, v_mod)], inc=sDV)
    S.emit("vector", lambda v: v.tensor_scalar(cvall("cb1"), MODV[:, 64:80],
                                               1.0, None, op0=ALU.add), inc=sDV)
    v_modpost = sDV.cnt
    for src0, dst in ((32, "gac"), (80, "gbc")):
        g = MODV[:, src0:src0 + 16]
        S.emit("vector", lambda v, g_=g: v.tensor_scalar(
            PTX, g_, 0.25, None, op0=ALU.max), inc=sDV)
        S.emit("vector", lambda v, g_=g: v.tensor_scalar(
            AUX[:, 0:16], g_, -0.25, None, op0=ALU.min), inc=sDV)
        S.emit("vector", lambda v, g_=g: v.tensor_scalar(
            AUX[:, 16:32], g_, 0.0, None, op0=ALU.is_ge), inc=sDV)
        S.emit("vector", lambda v: v.tensor_tensor(
            PTX, PTX, AUX[:, 0:16], op=ALU.subtract), inc=sDV)
        S.emit("vector", lambda v: v.tensor_tensor(
            PTX, PTX, AUX[:, 16:32], op=ALU.mult), inc=sDV)
        v_modpost = S.emit("vector", lambda v, d=dst: v.tensor_tensor(
            cvall(d), AUX[:, 0:16], PTX, op=ALU.add), inc=sDV)

    stats_sq_guard = [0]

    def emit_stats_phase(src_chunk_fn, scr_big_bf, width, nblocks, src_waits,
                         src_waits_fn=None, slot_guard=True, presums=None):
        del scr_big_bf
        del nblocks
        nbws = [342, 342, width - 684]
        nblocks = 3

        def w0_of(nb):
            return sum(nbws[:nb])

        if presums is not None:
            sum_pe = list(presums)
        else:
            sum_pe = [0] * nblocks
        for nb in (() if presums is not None else range(nblocks)):
            nbw = nbws[nb]
            w0 = w0_of(nb)
            bk = 4 + nb
            wl0 = bank_wait(bk)
            for fc in range(FC):
                wls = src_waits_fn(fc) if src_waits_fn else src_waits
                waits = (wl0 + wls) if fc == 0 else wls
                r = S.emit(
                    "tensor",
                    lambda t, a=src_chunk_fn(fc), b=bk, st=(fc == 0),
                    sp=(fc == FC - 1), w0_=w0, n_=nbw:
                    t.matmul(psf[b][0:1, 0:n_], ONESFB[:, :],
                             a[:, w0_:w0_ + n_], start=st, stop=sp),
                    waits=waits, inc=(sPE if fc == FC - 1 else None))
                sum_pe[nb] = r if r is not None else sum_pe[nb]
        slot_mark = [stats_sq_guard[0], stats_sq_guard[0]]
        sq_pe = [0] * nblocks
        for fc in range(FC):
            slot = WKVT[:, 0:width] if fc % 2 == 0 else S0[:, 1:1 + width]
            wls = src_waits_fn(fc) if src_waits_fn else src_waits
            wl = list(wls) + [(sPE, slot_mark[fc % 2])]
            if fc < 2 and slot_guard:
                wl += list(src_waits)
            a1 = S.emit("scalar", lambda sc, s=src_chunk_fn(fc), d=slot:
                        sc.activation(d, s, AF.Square, bias=0.0, scale=1.0),
                        waits=wl, inc=sAC)
            last = 0
            for nb in range(nblocks):
                nbw = nbws[nb]
                w0 = w0_of(nb)
                bk = nb
                waits = [(sAC, a1)] + (bank_wait(bk) if fc == 0 else [])
                r = S.emit(
                    "tensor",
                    lambda t, d=slot, b=bk, st=(fc == 0), sp=(fc == FC - 1),
                    w0_=w0, n_=nbw:
                    t.matmul(psf[b][0:1, 0:n_], ONESFB[:, :],
                             d[:, w0_:w0_ + n_], start=st, stop=sp),
                    waits=waits,
                    inc=(sPE if nb == nblocks - 1 or fc == FC - 1 else None))
                if r is not None:
                    last = r
                    sq_pe[nb] = max(sq_pe[nb], r)
            slot_mark[fc % 2] = last
        stats_sq_guard[0] = max(slot_mark)
        v_bc = 0
        prev_bc_pe = 0
        for nb in range(nblocks):
            nbw = nbws[nb]
            w0 = w0_of(nb)
            Cm = STATROW[:, 0:nbw]
            Bv = STATROW[:, 342:342 + nbw]
            Dr = STATROW[:, 684:684 + nbw]
            wl1 = [(sPE, sum_pe[nb])]
            if prev_bc_pe:
                wl1.append((sPE, prev_bc_pe))
            dv1 = S.emit("vector", lambda v, b=4 + nb, c_=Cm, n_=nbw:
                v.tensor_scalar(c_, psf[b][0:1, 0:n_], IH, None, op0=ALU.mult),
                waits=wl1, inc=sDV)
            dv2 = S.emit("vector", lambda v, b=nb, bv=Bv, n_=nbw:
                v.tensor_scalar(bv, psf[b][0:1, 0:n_], IH, None, op0=ALU.mult),
                waits=[(sPE, sq_pe[nb])], inc=sDV)
            S.emit("vector", lambda v, d=Dr, c_=Cm: v.tensor_tensor(
                d, c_, c_, op=ALU.mult), inc=sDV)
            dv4 = S.emit("vector", lambda v, bv=Bv, d=Dr: v.tensor_tensor(
                bv, bv, d, op=ALU.subtract), inc=sDV)
            S.emit("scalar", lambda sc, d=Dr, bv=Bv: sc.activation(
                d, bv, AF.Ln, bias=EPSC[:, :], scale=1.0),
                waits=[(sDV, dv4)], inc=sAC)
            ac2 = S.emit("scalar", lambda sc, d=Dr: sc.activation(
                d, d, AF.Exp, bias=0.0, scale=-0.5), inc=sAC)
            dv5 = S.emit("vector", lambda v, c_=Cm, d=Dr: v.tensor_tensor(
                c_, c_, d, op=ALU.mult), waits=[(sAC, ac2)], inc=sDV)
            bank_done(4 + nb, sDV, dv1)
            bank_done(nb, sDV, dv2)
            for which, srcslot in ((0, Dr), (1, Cm)):
                bkb = 3
                wl = bank_wait(bkb) + [(sDV, dv5)]
                pe_v = S.emit("tensor", lambda t, b=bkb, s=srcslot, n_=nbw:
                              t.matmul(psf[b][:, 0:n_], ONESF[0:1, :], s,
                                       start=True, stop=True), waits=wl, inc=sPE)
                prev_bc_pe = pe_v
                v_bc = S.emit("scalar", lambda sc, b=bkb, wh=which, w0_=w0, n_=nbw:
                              sc.copy(BCB[:, wh * W + w0_: wh * W + w0_ + n_],
                                      psf[b][:, 0:n_]),
                              waits=[(sPE, pe_v)], inc=sAC)
                bank_done(bkb, sAC, v_bc)
        return v_bc

    v_lna = emit_stats_phase(lambda fc: B1f[:, fc * W: fc * W + W],
                             B2b, W, 3, [(sAD, v_x0)], presums=lna_sum_pe)

    v_h = 0
    h_marks = []
    for fc in range(FC):
        src = B1f[:, fc * W: fc * W + W]
        dst = B2f[:, fc * W: fc * W + W]
        S.emit("vector", lambda v, s=src: v.tensor_tensor(
            DT[:, 0:W], s, BCB[:, 0:W], op=ALU.mult),
            waits=[(sAC, v_lna), (sDV, v_modpost)], inc=sDV)
        S.emit("vector", lambda v: v.tensor_tensor(
            DT[:, 0:W], DT[:, 0:W], BCB[:, W:2 * W], op=ALU.subtract), inc=sDV)
        v_h = S.emit("vector", lambda v, d=dst, f=fc: v.tensor_scalar(
            d, DT[:, 0:W], cv("ca1", f), MODV[:, f:f + 1],
            op0=ALU.mult, op1=ALU.add), inc=sDV)
        h_marks.append(v_h)

    v_ln1 = emit_stats_phase(lambda fc: B2f[:, fc * W: fc * W + W],
                             B1b, W, 3, [(sDV, v_h)],
                             src_waits_fn=lambda fc: [(sDV, h_marks[fc])],
                             slot_guard=False)

    def mixA(fc, width=W):
        return B1b[:, fc * W: fc * W + width]

    def mixB(fc, width=W):
        return B1b[:, FCW + fc * W: FCW + fc * W + width]

    def emit_mix_chunk(fc, v_bcast, src_big, gslot, bslot, mixes, width=W,
                       extra_waits=()):
        last = 0
        if True:
            src = src_big[:, fc * W: fc * W + W]
            wl0 = [(sAC, v_bcast), (sAD, v_hmsk)] + list(extra_waits)
            S.emit("vector", lambda v, s=src: v.tensor_tensor(
                S0B[:, 1:W + 1], s, BCB[:, 0:W], op=ALU.mult),
                waits=wl0, inc=sDV)
            S.emit("vector", lambda v: v.tensor_tensor(
                S0B[:, 1:W + 1], S0B[:, 1:W + 1], BCB[:, W:2 * W],
                op=ALU.subtract), inc=sDV)
            S.emit("vector", lambda v, f=fc, g=gslot, b_=bslot: v.tensor_scalar(
                S0B[:, 1:W + 1], S0B[:, 1:W + 1], cv(g, f), cv(b_, f),
                op0=ALU.mult, op1=ALU.add), inc=sDV)
            S.emit("vector", lambda v: v.tensor_scalar(
                S0B[:, 2:3], S0B[:, 2:3], HMSK[:, 0:1], None, op0=ALU.mult),
                inc=sDV)
            S.emit("vector", lambda v: v.tensor_tensor(
                DT[:, 0:W], S0B[:, 1:W + 1], S0B[:, 0:W], op=ALU.subtract),
                inc=sDV)
            for slotname, dest_fn in mixes:
                if width == W:
                    last = S.emit("vector", lambda v, f=fc, sn=slotname, df=dest_fn:
                                  v.scalar_tensor_tensor(
                                      df(f), DT[:, 0:W], cv(sn, f), S0B[:, 0:W],
                                      op0=ALU.mult, op1=ALU.add), inc=sDV)
                else:
                    last = S.emit("vector", lambda v, f=fc, sn=slotname, df=dest_fn:
                                  v.scalar_tensor_tensor(
                                      df(f), DT[:, 2:2 + NT], cv(sn, f),
                                      S0B[:, 2:2 + NT], op0=ALU.mult, op1=ALU.add),
                                  inc=sDV)
        return last

    def emit_mix_pair(v_bcast, src_big, gslot, bslot, mixes, width=W,
                      extra_waits=()):
        last = 0
        for fc in range(FC):
            last = emit_mix_chunk(fc, v_bcast, src_big, gslot, bslot, mixes,
                                  width, extra_waits if fc == 0 else ())
        return last

    v_mix_rk = emit_mix_pair(v_ln1, B2f, "g1", "b1",
                             [("mr", lambda fc: mixA(fc)),
                              ("mk", lambda fc: mixB(fc))])

    def emit_hxh(wname, moving_fn, evac_fn, v_moving, nblk=3, width=W,
                 extra_mm_waits=(), after_mc=None):
        nbw = width // nblk
        for mc in range(FC):
            slot_ap, rdy, slot = wq_dma(dt[wname][mc, :, :])
            pe_last = None
            for nb in range(nblk):
                c0 = nb * nbw
                bank = (mc * nblk + nb) % 4
                wl = [rdy, (sDV, v_moving)] + bank_wait(bank) + \
                    list(extra_mm_waits)
                for k in range(FC):
                    r = S.emit(
                        "tensor",
                        lambda t, sl=slot_ap, k_=k, b=bank, c=c0,
                        st=(k == 0), sp=(k == FC - 1), mf=moving_fn:
                        t.matmul(psf[b][:, 0:nbw], sl[:, k_ * 128:(k_ + 1) * 128],
                                 mf(k_, c, nbw), start=st, stop=sp),
                        waits=(wl if k == 0 else []),
                        inc=(sPE if k == FC - 1 else None))
                    pe_last = r if r is not None else pe_last
                sem, val = evac_fn(mc, nb, c0, nbw, psf[bank], [(sPE, pe_last)])
                bank_done(bank, sem, val)
            wq_slot_pe[slot] = pe_last
            if after_mc is not None:
                after_mc(mc)

    def evac_sigr(mc, nb, c0, cw, ps, waits):
        v = S.emit("scalar", lambda sc, m=mc, c=c0, w=cw, p=ps:
                   sc.activation(B3b[:, m * W + c: m * W + c + w], p[:, 0:w],
                                 AF.Sigmoid, bias=0.0, scale=1.0),
                   waits=waits, inc=sAC)
        return sAC, v

    emit_hxh("wr", lambda k, c0, nw: mixA(k)[:, c0:c0 + nw], evac_sigr, v_mix_rk)
    v_sigr = sAC.cnt
    v_r_pe = sPE.cnt

    def evac_wkv(mc, nb, c0, cw, ps, waits):
        a1 = S.emit("scalar", lambda sc, p=ps, m=mc, w=cw:
                    sc.activation(WKVT[:, 0:w], p[:, 0:w], AF.Identity,
                                  bias=cv("tmf", m), scale=1.0),
                    waits=waits + [(sDV, sDV.cnt), (sPE, sPE.cnt)], inc=sAC)
        S.emit("scalar", lambda sc, w=cw: sc.activation(
            WKVT[:, 342:342 + w], WKVT[:, 0:w], AF.Relu, bias=NEGC[:, :],
            scale=-1.0), inc=sAC)
        S.emit("scalar", lambda sc, w=cw: sc.activation(
            GATEB[:, 0:w], WKVT[:, 342:342 + w], AF.Exp, bias=0.0, scale=-1.0),
            inc=sAC)
        a3 = S.emit("scalar", lambda sc, w=cw: sc.activation(
            GATEB[:, 342:342 + w], WKVT[:, 342:342 + w], AF.Exp, bias=0.0,
            scale=1.0), inc=sAC)
        S.emit("vector", lambda v, w=cw: v.tensor_tensor(
            GATEB[:, 0:w], GATEB[:, 0:w], GATEB[:, 342:342 + w], op=ALU.mult),
            waits=[(sAC, a3)], inc=sDV)
        d4 = S.emit("vector", lambda v, m=mc, c=c0, w=cw: v.tensor_tensor(
            B3b[:, m * W + c: m * W + c + w], B3b[:, m * W + c: m * W + c + w],
            GATEB[:, 0:w], op=ALU.mult), inc=sDV)
        return sDV, d4

    mixv_last = [0]

    def _mixv_chunk(mc):
        mixv_last[0] = emit_mix_chunk(
            mc, v_ln1, B2f, "g1", "b1", [("mv", lambda fc: mixA(fc))],
            extra_waits=([(sPE, v_r_pe)] if mc == 0 else ()))

    emit_hxh("wk", lambda k, c0, nw: mixB(k)[:, c0:c0 + nw], evac_wkv, v_mix_rk,
             extra_mm_waits=[(sAC, v_sigr)], after_mc=_mixv_chunk)
    v_gate = sDV.cnt
    v_mix_v = mixv_last[0]

    def evac_gin(mc, nb, c0, cw, ps, waits):
        v = S.emit("vector", lambda vv, m=mc, c=c0, w=cw, p=ps:
                   vv.tensor_tensor(mixB(m)[:, c: c + w], p[:, 0:w],
                                    B3b[:, m * W + c: m * W + c + w], op=ALU.mult),
                   waits=waits + [(sDV, v_gate)], inc=sDV)
        return sDV, v

    emit_hxh("wv", lambda k, c0, nw: mixA(k)[:, c0:c0 + nw], evac_gin, v_mix_v)
    v_gin = sDV.cnt

    xa_marks = [0] * FC

    def evac_xa(mc, nb, c0, cw, ps, waits):
        v = S.emit("vector", lambda vv, m=mc, c=c0, w=cw, p=ps:
                   vv.tensor_tensor(B2f[:, m * W + c: m * W + c + w],
                                    B2f[:, m * W + c: m * W + c + w],
                                    p[:, 0:w], op=ALU.add), waits=waits, inc=sDV)
        xa_marks[mc] = v
        return sDV, v

    emit_hxh("wo", lambda k, c0, nw: mixB(k)[:, c0:c0 + nw], evac_xa, v_gin)
    v_xa = sDV.cnt

    v_ln2 = emit_stats_phase(lambda fc: B2f[:, fc * W: fc * W + W],
                             B1b, W, 3, [(sDV, v_xa)],
                             src_waits_fn=lambda fc: [(sDV, xa_marks[fc])],
                             slot_guard=False)

    v_mix_ck = emit_mix_pair(v_ln2, B2f, "g2", "b2",
                             [("cmr", lambda fc: mixA(fc, NT)),
                              ("cmk", lambda fc: mixB(fc, NT))], width=NT)

    def evac_scr(mc, nb, c0, cw, ps, waits):
        v = S.emit("scalar", lambda sc, m=mc, c=c0, w=cw, p=ps:
                   sc.activation(B3b[:, m * NT + c: m * NT + c + w], p[:, 0:w],
                                 AF.Sigmoid, bias=0.0, scale=1.0),
                   waits=waits, inc=sAC)
        return sAC, v

    emit_hxh("wcr", lambda k, c0, nw: mixA(k, NT)[:, c0:c0 + nw], evac_scr,
             v_mix_ck, nblk=2, width=NT, extra_mm_waits=[(sDV, v_gin)])
    v_scr = sAC.cnt

    v_xasp = sync_dma(lambda s: s.dma_start(
        dt["xasp"][:, :].rearrange("(f p) t -> p f t", p=128),
        B2f[:, :].rearrange("p (f t) -> p f t", t=W)[:, :, 2:W]),
        waits=[(sDV, v_mix_ck)])

    v_xb = [0, 0]
    v_xbsp = 0
    for tb in range(2):
        c0tb = tb * TB
        wl = [(sAD, v_xasp), (sAC, v_scr)]
        if tb == 1:
            wl.append((sAD, v_xbsp))
        v_xare = sync_dma(lambda s, c=c0tb: s.dma_start(
            B1f[:, 0:FC * TB].rearrange("p (f t) -> p f t", t=TB),
            dt["xasp"][:, :].rearrange("(f p) t -> p f t", p=128)[:, :, c:c + TB]),
            waits=wl)

        v_skk = 0
        for mg in range(DC4):
            slot_ap, rdy, slot = wq_dma(dt["wkey"][mg, :, :])
            bank = mg % 4
            wl2 = [rdy, (sDV, v_mix_ck), (sAD, v_xasp)] + bank_wait(bank)
            pe_last = None
            for k in range(FC):
                r = S.emit(
                    "tensor",
                    lambda t, sl=slot_ap, k_=k, b=bank, c=c0tb,
                    st=(k == 0), sp=(k == FC - 1):
                    t.matmul(psf[b][:, 0:TB], sl[:, k_ * 128:(k_ + 1) * 128],
                             mixB(k_, NT)[:, c:c + TB], start=st, stop=sp),
                    waits=(wl2 if k == 0 else []),
                    inc=(sPE if k == FC - 1 else None))
                pe_last = r if r is not None else pe_last
            wq_slot_pe[slot] = pe_last
            v_skk = S.emit("scalar", lambda sc, b=bank, m=mg:
                           sc.activation(B2b[:, m * TB:(m + 1) * TB],
                                         psf[b][:, 0:TB], AF.Silu,
                                         bias=0.0, scale=1.0),
                           waits=[(sPE, pe_last)], inc=sAC)
            bank_done(bank, sAC, v_skk)

        for m in range(FC):
            bank = m % 4
            first = True
            pe_last = None
            for kq in range(4):
                slot_ap, rdy, slot = wq_dma(dt["wval"][m, :, kq * 2048:(kq + 1) * 2048])
                wl3 = [rdy, (sAC, v_skk)] + (bank_wait(bank) if first else [])
                for kk in range(16):
                    g = kq * 16 + kk
                    r = S.emit(
                        "tensor",
                        lambda t, sl=slot_ap, k_=kk, g_=g, b=bank,
                        st=first, sp=(kq == 3 and kk == 15):
                        t.matmul(psf[b][:, 0:TB], sl[:, k_ * 128:(k_ + 1) * 128],
                                 B2b[:, g_ * TB:(g_ + 1) * TB], start=st, stop=sp),
                        waits=(wl3 if kk == 0 else []),
                        inc=(sPE if kk == 15 else None))
                    pe_last = r if r is not None else pe_last
                    first = False
                wq_slot_pe[slot] = pe_last
            S.emit("vector", lambda v, b=bank, m_=m, c=c0tb:
                   v.tensor_tensor(S1[:, 0:TB], psf[b][:, 0:TB],
                                   B3b[:, m_ * NT + c: m_ * NT + c + TB],
                                   op=ALU.mult),
                   waits=[(sPE, pe_last), (sAC, v_scr), (sAD, v_xare)], inc=sDV)
            dv = S.emit("vector", lambda v, m_=m:
                        v.tensor_tensor(B1f[:, m_ * TB:(m_ + 1) * TB], S1[:, 0:TB],
                                        B1f[:, m_ * TB:(m_ + 1) * TB], op=ALU.add),
                        inc=sDV)
            bank_done(bank, sDV, dv)
        v_xb[tb] = sDV.cnt

        if tb == 0:
            v_xbsp = sync_dma(lambda s: s.dma_start(
                dt["xbsp"][:, :].rearrange("(f p) t -> p f t", p=128),
                B1f[:, 0:FC * TB].rearrange("p (f t) -> p f t", t=TB)),
                waits=[(sDV, v_xb[0])])

    v_xb1cp = 0
    for fc in range(FC):
        v_xb1cp = S.emit("vector", lambda v, f=fc: v.tensor_copy(
            B2f[:, f * W + 2 + TB: f * W + 2 + 2 * TB], B1f[:, f * TB:(f + 1) * TB]),
            waits=[(sDV, v_xb[1])], inc=sDV)
    v_xbre = scal_dma(lambda s: s.dma_start(
        B2f[:, :].rearrange("p (f t) -> p f t", t=W)[:, :, 2:2 + TB],
        dt["xbsp"][:, :].rearrange("(f p) t -> p f t", p=128)),
        waits=[(sDV, v_xb1cp)])
    v_x0re = sync_dma(lambda s: s.dma_start(
        B1f[:, 0:FC * NT].rearrange("p (f t) -> p f t", t=NT),
        dt["x0"][:, :].rearrange("(f p) t -> p f t", p=128)[:, :, 2:W]),
        waits=[(sDV, v_xb1cp)])

    v_x1 = 0
    x1_marks = []
    for fc in range(FC):
        v_x1 = S.emit("vector", lambda v, f=fc: v.scalar_tensor_tensor(
            B2f[:, f * W + 2: f * W + 2 + NT],
            B2f[:, f * W + 2: f * W + 2 + NT], cv("gac", f),
            B1f[:, f * NT:(f + 1) * NT], op0=ALU.mult, op1=ALU.add),
            waits=[(sAD2, v_xbre), (sAD, v_x0re)], inc=sDV)
        x1_marks.append(v_x1)

    v_lna2 = emit_stats_phase(lambda fc: B2f[:, fc * W + 2: fc * W + 2 + NT],
                              B1b, NT, 2, [(sDV, v_x1)],
                              src_waits_fn=lambda fc: [(sDV, x1_marks[fc])],
                              slot_guard=False)

    v_hf = 0
    hf_marks = []
    for fc in range(FC):
        src = B2f[:, fc * W + 2: fc * W + 2 + NT]
        S.emit("vector", lambda v, s=src: v.tensor_tensor(
            DT[:, 0:NT], s, BCB[:, 0:NT], op=ALU.mult),
            waits=[(sAC, v_lna2)], inc=sDV)
        S.emit("vector", lambda v: v.tensor_tensor(
            DT[:, 0:NT], DT[:, 0:NT], BCB[:, W:W + NT], op=ALU.subtract), inc=sDV)
        v_hf = S.emit("vector", lambda v, f=fc: v.tensor_scalar(
            B3b[:, f * NT:(f + 1) * NT], DT[:, 0:NT], cv("cb1", f),
            MODV[:, 48 + f:49 + f], op0=ALU.mult, op1=ALU.add), inc=sDV)
        hf_marks.append(v_hf)

    v_logit = 0
    for nb in range(2):
        bk = aux_next()
        wl = [(sWR, v_wrt)] + bank_wait(bk)
        pe_last = None
        for k in range(FC):
            r = S.emit("tensor",
                       lambda t, k_=k, b=bk, c=nb * 512,
                       st=(k == 0), sp=(k == FC - 1):
                       t.matmul(psf[b][0:16, 0:512],
                                WRT[:, k_ * 16:(k_ + 1) * 16],
                                B3b[:, k_ * NT + c: k_ * NT + c + 512],
                                start=st, stop=sp),
                       waits=((wl if k == 0 else []) + [(sDV, hf_marks[k])]),
                       inc=(sPE if k == FC - 1 else None))
            pe_last = r if r is not None else pe_last
        v_logit = S.emit("scalar", lambda sc, b=bk, c=nb * 512:
                         sc.copy(SMALL[0:16, c:c + 512], psf[b][0:16, 0:512]),
                         waits=[(sPE, pe_last)], inc=sAC)
        bank_done(bk, sAC, v_logit)

    v_gatefm = 0
    for t8 in range(8):
        bk = aux_next()
        wl = [(sAC, v_logit)] + bank_wait(bk)
        pe_v = S.emit("tensor", lambda t, b=bk, i=t8: t.transpose(
            psf[b][0:128, 0:16], SMALL[0:16, i * 128:(i + 1) * 128],
            IDF[0:16, 0:16]), waits=wl, inc=sPE)
        ac_v = S.emit("scalar", lambda sc, b=bk, i=t8: sc.copy(
            PT[:, i * 16:(i + 1) * 16], psf[b][0:128, 0:16]),
            waits=[(sPE, pe_v)], inc=sAC)
        bank_done(bk, sAC, ac_v)
        tile = PT[:, t8 * 16:(t8 + 1) * 16]
        gtile = tile
        c = t8
        S.emit("vector", lambda v, tl=tile, c_=c: v.reduce_max(
            AUX[:, c_:c_ + 1], tl, axis=AX.X), waits=[(sAC, ac_v)], inc=sDV)
        nmx = S.emit("vector", lambda v, c_=c: v.tensor_scalar(
            AUX[:, 8 + c_:9 + c_], AUX[:, c_:c_ + 1], -1.0, None, op0=ALU.mult),
            inc=sDV)
        a1 = S.emit("scalar", lambda sc, tl=tile, c_=c: sc.activation(
            tl, tl, AF.Exp, bias=AUX[:, 8 + c_:9 + c_], scale=1.0),
            waits=[(sDV, nmx)], inc=sAC)
        S.emit("vector", lambda v, tl=tile, c_=c: v.reduce_sum(
            AUX[:, 16 + c_:17 + c_], tl, axis=AX.X), waits=[(sAC, a1)], inc=sDV)
        S.emit("vector", lambda v, c_=c: v.reciprocal(
            AUX[:, 24 + c_:25 + c_], AUX[:, 16 + c_:17 + c_]), inc=sDV)
        S.emit("vector", lambda v, tl=tile, c_=c: v.tensor_scalar(
            tl, tl, AUX[:, 24 + c_:25 + c_], None, op0=ALU.mult), inc=sDV)
        S.emit("vector", lambda v, tl=tile, c_=c: v.reduce_max(
            AUX[:, 32 + c_:33 + c_], tl, axis=AX.X), inc=sDV)
        S.emit("vector", lambda v, tl=tile, c_=c: v.tensor_scalar(
            PTX, tl, AUX[:, 32 + c_:33 + c_], None, op0=ALU.is_lt),
            inc=sDV)
        S.emit("vector", lambda v, tl=tile: v.tensor_tensor(
            PTX, tl, PTX, op=ALU.mult), inc=sDV)
        S.emit("vector", lambda v, c_=c: v.reduce_max(
            AUX[:, 40 + c_:41 + c_], PTX, axis=AX.X), inc=sDV)
        S.emit("vector", lambda v, tl=tile, c_=c: v.tensor_scalar(
            PTX, tl, AUX[:, 40 + c_:41 + c_], None, op0=ALU.is_ge),
            inc=sDV)
        dvg = S.emit("vector", lambda v, tl=tile, g=gtile: v.tensor_tensor(
            g, tl, PTX, op=ALU.mult), inc=sDV)
        bk2 = aux_next()
        wl2 = [(sDV, dvg)] + bank_wait(bk2)
        pe2 = S.emit("tensor", lambda t, b=bk2, g=gtile: t.transpose(
            psf[b][0:16, 0:128], g, IDF[:, :]), waits=wl2, inc=sPE)
        v_gatefm = S.emit("scalar", lambda sc, b=bk2, i=t8: sc.copy(
            SMALL[0:16, i * 128:(i + 1) * 128], psf[b][0:16, 0:128]),
            waits=[(sPE, pe2)], inc=sAC)
        bank_done(bk2, sAC, v_gatefm)

    v_x1sp = sync_dma(lambda s: s.dma_start(
        dt["x1sp"][:, :].rearrange("(f p) t -> p f t", p=128),
        B2f[:, :].rearrange("p (f t) -> p f t", t=W)[:, :, 2:W]),
        waits=[(sDV, v_x1)])

    GROUPS = [[("sh", 0), ("sh", 1), ("rt", 0), ("rt", 1)],
              [("rt", 2), ("rt", 3), ("rt", 4), ("rt", 5)],
              [("rt", 6), ("rt", 7), ("rt", 8), ("rt", 9)],
              [("rt", 10), ("rt", 11), ("rt", 12), ("rt", 13)],
              [("rt", 14), ("rt", 15)]]

    def mid_ap(q, dc, c0, cw):
        base = q * (DCE * NT) + dc * NT + c0
        return B1b[:, base: base + cw]

    aux_banks[:] = [6]
    racc_started = False
    gateb_state = [None]
    selw_pe = [0]
    first_up = [True]

    for gi, grp in enumerate(GROUPS):
        gb_ready = {}
        grp_mid_sem = None
        grp_mid_val = 0
        for qi, (kind, e) in enumerate(grp):
            if kind == "rt":
                slot_g = 0
                wl0 = [(sGP, v_iota)]
                if selw_pe[0]:
                    wl0.append((sPE, selw_pe[0]))
                dv_sel = S.emit("vector", lambda v, e_=e: v.tensor_scalar(
                    SELW[:, :], R16[:, :], float(e_), None, op0=ALU.is_equal),
                    waits=wl0, inc=sDV)
                for nb in range(2):
                    bk = aux_next()
                    wl = [(sAC, v_gatefm), (sDV, dv_sel)] + bank_wait(bk)
                    if gateb_state[slot_g] is not None:
                        wl.append(gateb_state[slot_g])
                    pe_v = S.emit("tensor", lambda t, b=bk, c=nb * 512:
                                  t.matmul(psf[b][:, 0:512], SELW[:, :],
                                           SMALL[0:16, c:c + 512],
                                           start=True, stop=True), waits=wl, inc=sPE)
                    selw_pe[0] = pe_v
                    ac_v = S.emit("scalar", lambda sc, b=bk, c=nb * 512:
                                  sc.copy(GATEB[:, c: c + 512],
                                          psf[b][:, 0:512]),
                                  waits=[(sPE, pe_v)], inc=sAC)
                    bank_done(bk, sAC, ac_v)
                gb_ready[e] = sAC.cnt
            wnm = "wshu" if kind == "sh" else "wrtu"
            exp_last = None
            for dc in range(DCE):
                slot_ap, rdy, slot = wq_dma(dt[wnm][e * DCE + dc, :, :])
                for nb in range(2):
                    c0 = nb * 512
                    bank = (qi * DCE * 2 + dc * 2 + nb) % 4
                    wl = [rdy] + bank_wait(bank)
                    if first_up[0]:
                        wl += [(sDV, v_x1), (sAC, v_lna2)]
                        first_up[0] = False
                    pe_last = None
                    for k in range(FC):
                        r = S.emit(
                            "tensor",
                            lambda t, sl=slot_ap, k_=k, b=bank, c=c0,
                            st=(k == 0), sp=(k == FC - 1):
                            t.matmul(psf[b][:, 0:512],
                                     sl[:, k_ * 128:(k_ + 1) * 128],
                                     B3b[:, k_ * NT + c: k_ * NT + c + 512],
                                     start=st, stop=sp),
                            waits=((wl if k == 0 else []) +
                                   [(sDV, hf_marks[k])]),
                            inc=(sPE if k == FC - 1 else None))
                        pe_last = r if r is not None else pe_last
                    ac_v = S.emit("scalar", lambda sc, b=bank, q=qi, d=dc, c=c0:
                                  sc.activation(mid_ap(q, d, c, 512),
                                                psf[b][:, 0:512], AF.Gelu,
                                                bias=0.0, scale=1.0),
                                  waits=[(sPE, pe_last)], inc=sAC)
                    if kind == "rt":
                        dv_v = S.emit("vector",
                                      lambda v, q=qi, d=dc, c=c0:
                                      v.tensor_tensor(
                                          mid_ap(q, d, c, 512),
                                          mid_ap(q, d, c, 512),
                                          GATEB[:, c: c + 512],
                                          op=ALU.mult),
                                      waits=[(sAC, ac_v), (sAC, gb_ready[e])],
                                      inc=sDV)
                        bank_done(bank, sDV, dv_v)
                        grp_mid_sem, grp_mid_val = sDV, dv_v
                        exp_last = (sDV, dv_v)
                    else:
                        bank_done(bank, sAC, ac_v)
                        grp_mid_sem, grp_mid_val = sAC, ac_v
                wq_slot_pe[slot] = sPE.cnt
            if kind == "rt":
                gateb_state[0] = exp_last
        mid_wait = (grp_mid_sem, grp_mid_val)

        for mc in range(FC):
            first = [True, True]
            pe_last2 = [None, None]
            for qi, (kind, e) in enumerate(grp):
                wnm = "wshd" if kind == "sh" else "wrtd"
                slot_ap, rdy, slot = wq_dma(dt[wnm][e * FC + mc, :, :])
                for nb in range(2):
                    bank = 4 + nb
                    c0 = nb * 512
                    wl = [rdy, mid_wait]
                    if first[nb]:
                        wl += bank_wait(bank)
                    for k in range(DCE):
                        r = S.emit(
                            "tensor",
                            lambda t, sl=slot_ap, k_=k, b=bank, q=qi, c=c0,
                            st=first[nb],
                            sp=(qi == len(grp) - 1 and k == DCE - 1):
                            t.matmul(psf[b][:, 0:512],
                                     sl[:, k_ * 128:(k_ + 1) * 128],
                                     mid_ap(q, k_, c, 512), start=st, stop=sp),
                            waits=(wl if k == 0 else []),
                            inc=(sPE if k == DCE - 1 else None))
                        pe_last2[nb] = r if r is not None else pe_last2[nb]
                        first[nb] = False
                wq_slot_pe[slot] = sPE.cnt
            for nb in range(2):
                bank = 4 + nb
                c0 = nb * 512
                dst = B2f[:, mc * NT + c0: mc * NT + c0 + 512]
                if not racc_started:
                    wl = [(sPE, pe_last2[nb]), (sAD, v_x1sp), (sDV, v_hf),
                          (sAC, v_lna2)]
                    dv = S.emit("vector", lambda v, b=bank, d=dst:
                                v.tensor_copy(d, psf[b][:, 0:512]),
                                waits=wl, inc=sDV)
                else:
                    dv = S.emit("vector", lambda v, b=bank, d=dst:
                                v.tensor_tensor(d, d, psf[b][:, 0:512],
                                                op=ALU.add),
                                waits=[(sPE, pe_last2[nb])], inc=sDV)
                bank_done(bank, sDV, dv)
        racc_started = True
    v_racc = sDV.cnt

    v_x1re = sync_dma(lambda s: s.dma_start(
        B1f[:, 0:FC * NT].rearrange("p (f t) -> p f t", t=NT),
        dt["x1sp"][:, :].rearrange("(f p) t -> p f t", p=128)),
        waits=[(sDV, v_racc)])

    v_fin = 0
    for fc in range(FC):
        v_fin = S.emit("vector", lambda v, f=fc: v.scalar_tensor_tensor(
            B2f[:, f * NT:(f + 1) * NT], B2f[:, f * NT:(f + 1) * NT],
            cv("gbc", f), B1f[:, f * NT:(f + 1) * NT],
            op0=ALU.mult, op1=ALU.add),
            waits=[(sAD, v_x1re), (sDV, v_racc)], inc=sDV)
    sync_dma(lambda s: s.dma_start(
        dt["out"][:, :].rearrange("(f p) t -> p f t", p=128),
        B2f[:, 0:FC * NT].rearrange("p (f t) -> p f t", t=NT)),
        waits=[(sDV, v_fin)])

    with ctx:
        handles = {}
        for s in S.sems:
            handles[s.name] = ctx.enter_context(nc.semaphore(s.name))
        with nc.Block() as block:
            own_sem = {"vector": sDV, "scalar": sAC, "gpsimd": sGP}

            def make_section(eng_name):
                own = own_sem.get(eng_name)

                def run(eng):
                    own_cnt = 0
                    for waits, fn, inc, inc_n in S.steps[eng_name]:
                        if own is not None and own_cnt > 0:
                            eng.wait_ge(handles[own.name], own_cnt)
                        for sm, val in waits:
                            eng.wait_ge(handles[sm.name], val)
                        inst = fn(eng)
                        if inc is not None:
                            inst.then_inc(handles[inc.name], inc_n)
                            if inc is own:
                                own_cnt += inc_n
                return run
            block.tensor(make_section("tensor"))
            block.vector(make_section("vector"))
            block.scalar(make_section("scalar"))
            block.sync(make_section("sync"))
            block.gpsimd(make_section("gpsimd"))
    return nc


def _prep(inputs):
    f32 = np.float32
    x = np.asarray(inputs["L0_attn_x"], f32)
    vecs = np.stack([
        np.asarray(inputs["L0_attn_tm_mk"], f32).reshape(H),
        np.asarray(inputs["L0_attn_tm_mv"], f32).reshape(H),
        np.asarray(inputs["L0_attn_tm_mr"], f32).reshape(H),
        np.asarray(inputs["L0_attn_cm_mk"], f32).reshape(H),
        np.asarray(inputs["L0_attn_cm_mr"], f32).reshape(H),
        np.asarray(inputs["L0_attn_ln1_g"], f32).reshape(H),
        np.asarray(inputs["L0_attn_ln1_b"], f32).reshape(H),
        np.asarray(inputs["L0_attn_ln2_g"], f32).reshape(H),
        np.asarray(inputs["L0_attn_ln2_b"], f32).reshape(H),
        np.asarray(inputs["L0_attn_tm_first"], f32).reshape(H),
    ])
    shared = {
        "vecs": np.ascontiguousarray(vecs),
        "wmod": _tile_kxm(np.asarray(inputs["L0_wmod"], f32)),
        "wr": _tile_kxm(np.asarray(inputs["L0_attn_Wr"], f32).T),
        "wk": _tile_kxm(np.asarray(inputs["L0_attn_Wk"], f32).T),
        "wv": _tile_kxm(np.asarray(inputs["L0_attn_Wv"], f32).T),
        "wo": _tile_kxm(np.asarray(inputs["L0_attn_Wo"], f32).T),
        "wcr": _tile_kxm(np.asarray(inputs["L0_attn_Wcr"], f32).T),
        "wkey": _tile_kxm(np.asarray(inputs["L0_attn_Wkey"], f32).T),
        "wval": _tile_kxm(np.asarray(inputs["L0_attn_Wval"], f32).T),
        "wrouter": np.ascontiguousarray(
            np.asarray(inputs["L0_ffn_W_router"], f32)
            .reshape(16, 128, 16).transpose(1, 0, 2).reshape(128, 256)),
        "wshu": np.concatenate([_tile_kxm(np.asarray(inputs["L0_ffn_Wsh_up"], f32)[e])
                                for e in range(E_SH)]),
        "wshd": np.concatenate([_tile_kxm(np.asarray(inputs["L0_ffn_Wsh_down"], f32)[e])
                                for e in range(E_SH)]),
        "wrtu": np.concatenate([_tile_kxm(np.asarray(inputs["L0_ffn_Wrt_up"], f32)[e])
                                for e in range(E_RT)]),
        "wrtd": np.concatenate([_tile_kxm(np.asarray(inputs["L0_ffn_Wrt_down"], f32)[e])
                                for e in range(E_RT)]),
    }
    c = np.asarray(inputs["L0_c"], f32)
    in_maps = []
    for core in range(8):
        b, s = core // 2, core % 2
        t0 = s * NT
        xc = np.zeros((W, H), f32)
        xc[2:] = x[b, t0:t0 + NT]
        if s == 1:
            xc[0:2] = x[b, t0 - 2:t0]
        m = dict(shared)
        m["x0"] = np.ascontiguousarray(xc.T)
        m["crow"] = np.ascontiguousarray(c[b].reshape(16, 128))
        m["hmask"] = np.full((128, 1), float(s), f32)
        in_maps.append(m)
    return in_maps


def kernel(**inputs):
    in_maps = _prep(inputs)
    nc = build_nc()
    res = run_bass_kernel_spmd(nc, in_maps, core_ids=list(range(8)))
    y = np.empty((B, T, H), np.float32)
    for core in range(8):
        b, s = core // 2, core % 2
        y[b, s * NT:(s + 1) * NT, :] = res.results[core]["out"].T
    return y


# version 14 (final)
# speedup vs baseline: 1.0067x; 1.0067x over previous
import contextlib
import numpy as np
import concourse.bass as bass
import concourse.mybir as mybir
from concourse.bass_utils import run_bass_kernel_spmd

F32 = mybir.dt.float32
BF16 = mybir.dt.bfloat16
AF = mybir.ActivationFunctionType
ALU = mybir.AluOpType
AX = mybir.AxisListType

H = 2048
B = 4
T = 2048
NT = 1024
W = NT + 2
FC = 16
D4 = 8192
DC4 = 64
EPS = 1e-5
E_RT = 16
E_SH = 2
DCE = 8
TB = 512
IH = 1.0 / H
FCW = FC * W

ENGINES = ("tensor", "vector", "scalar", "sync", "gpsimd")


class _Sem:
    __slots__ = ("name", "cnt")

    def __init__(self, name):
        self.name = name
        self.cnt = 0


class Sched:

    def __init__(self):
        self.steps = {e: [] for e in ENGINES}
        self.sems = []

    def sem(self, name):
        s = _Sem(name)
        self.sems.append(s)
        return s

    def emit(self, eng, fn, waits=(), inc=None, inc_n=1):
        w = []
        for s, v in waits:
            assert v <= s.cnt, f"wait on future: {s.name} {v} > {s.cnt} ({eng})"
            if v > 0:
                w.append((s, v))
        self.steps[eng].append((tuple(w), fn, inc, inc_n))
        if inc is not None:
            inc.cnt += inc_n
            return inc.cnt
        return None


def _tile_kxm(wT):
    K, M = wT.shape
    KC, MB = K // 128, M // 128
    a = wT.reshape(KC, 128, MB, 128).transpose(2, 1, 0, 3).reshape(MB, 128, KC * 128)
    return np.ascontiguousarray(a)


def build_nc():
    nc = bass.Bass("TRN2", target_bir_lowering=False)

    dt = {}
    dt["x0"] = nc.dram_tensor("x0", [H, W], F32, kind="ExternalInput")
    dt["crow"] = nc.dram_tensor("crow", [16, 128], F32, kind="ExternalInput")
    dt["hmask"] = nc.dram_tensor("hmask", [128, 1], F32, kind="ExternalInput")
    dt["vecs"] = nc.dram_tensor("vecs", [10, H], F32, kind="ExternalInput")
    dt["wmod"] = nc.dram_tensor("wmod", [96, 128, FC * 128], F32, kind="ExternalInput")
    for nm in ("wr", "wk", "wv", "wo", "wcr"):
        dt[nm] = nc.dram_tensor(nm, [16, 128, FC * 128], F32, kind="ExternalInput")
    dt["wkey"] = nc.dram_tensor("wkey", [64, 128, FC * 128], F32, kind="ExternalInput")
    dt["wval"] = nc.dram_tensor("wval", [16, 128, DC4 * 128], F32, kind="ExternalInput")
    dt["wrouter"] = nc.dram_tensor("wrouter", [128, 256], F32, kind="ExternalInput")
    dt["wshu"] = nc.dram_tensor("wshu", [E_SH * DCE, 128, FC * 128], F32, kind="ExternalInput")
    dt["wshd"] = nc.dram_tensor("wshd", [E_SH * FC, 128, DCE * 128], F32, kind="ExternalInput")
    dt["wrtu"] = nc.dram_tensor("wrtu", [E_RT * DCE, 128, FC * 128], F32, kind="ExternalInput")
    dt["wrtd"] = nc.dram_tensor("wrtd", [E_RT * FC, 128, DCE * 128], F32, kind="ExternalInput")
    dt["out"] = nc.dram_tensor("out", [H, NT], F32, kind="ExternalOutput")
    dt["xasp"] = nc.dram_tensor("xasp", [H, NT], F32, kind="Internal")
    dt["xbsp"] = nc.dram_tensor("xbsp", [H, TB], F32, kind="Internal")
    dt["x1sp"] = nc.dram_tensor("x1sp", [H, NT], F32, kind="Internal")

    S = Sched()
    sWDs = [S.sem(f"sWD{i}") for i in range(3)]
    sWR = S.sem("sWR")
    sAD = S.sem("sAD")
    sPE = S.sem("sPE")
    sDV = S.sem("sDV")
    sAC = S.sem("sAC")
    sGP = S.sem("sGP")

    ctx = contextlib.ExitStack()
    sb = lambda name, shape, dtyp: ctx.enter_context(nc.sbuf_tensor(name, shape, dtyp))
    psf = [ctx.enter_context(nc.psum_tensor(f"p{i}", [128, 512], F32)) for i in range(7)]
    p7b = ctx.enter_context(nc.psum_tensor("p7b", [128, 1024], BF16))

    B1f = sb("B1f", [128, FCW], F32)
    B2f = sb("B2f", [128, FCW], F32)
    B3b = sb("B3b", [128, FCW], BF16)
    B1bt = nc.alloc_sbuf_tensor_at("B1B", [128, 2 * FCW], BF16,
                                   offset=nc.lookup_mloc(B1f).addr)
    B2bt = nc.alloc_sbuf_tensor_at("B2B", [128, 2 * FCW], BF16,
                                   offset=nc.lookup_mloc(B2f).addr)
    B1b = B1bt[:, :]
    B2b = B2bt[:, :]
    S0 = sb("S0", [128, W + 1], F32)
    S1 = sb("S1", [128, W], F32)
    BCB = sb("BCB", [128, 2 * W], BF16)
    S0B = sb("S0B", [128, W + 1], BF16)
    DT = sb("DT", [128, W], BF16)
    SMALL = sb("SMALL", [16, W], F32)
    STATROW = SMALL[0:1, 0:1026]
    PT = sb("PT", [128, 128], F32)
    AUX = sb("AUX", [128, 64], F32)
    PTX = AUX[:, 48:64]
    CONSTV = sb("CONSTV", [128, 224], F32)
    MODV = sb("MODV", [128, 96], F32)
    WQ = sb("WQ", [128, 3 * 2048], BF16)
    WKVT = sb("WKVT", [128, 1026], F32)
    WRT = sb("WRT", [128, 256], BF16)
    GATEB = sb("GATEB", [128, NT], BF16)
    IDF = sb("IDF", [128, 128], F32)
    IDB = sb("IDB", [128, 128], BF16)
    ONESF = sb("ONESF", [1, 128], F32)
    ONESB = sb("ONESB", [128, 1], BF16)
    ONESFB = sb("ONESFB", [128, 1], F32)
    SELW = sb("SELW", [16, 128], F32)
    R16 = sb("R16", [16, 128], F32)
    EPSC = sb("EPSC", [1, 1], F32)
    NEGC = sb("NEGC", [128, 1], F32)
    SSB = sb("SSB", [128, 16], BF16)
    SSTGT = sb("SSTG", [16, 128], BF16)
    HMSK = sb("HMSK", [128, 1], F32)

    CSLOT = {"mk": 0, "mv": 1, "mr": 2, "cmk": 3, "cmr": 4, "g1": 5, "b1": 6,
             "g2": 7, "b2": 8, "tmf": 9, "ca1": 10, "cb1": 11, "gac": 12,
             "gbc": 13}

    def cv(slot, fc):
        j = CSLOT[slot]
        return CONSTV[:, 16 * j + fc: 16 * j + fc + 1]

    def cvall(slot):
        j = CSLOT[slot]
        return CONSTV[:, 16 * j: 16 * j + 16]

    wq_idx = [0]
    wq_slot_pe = [0, 0, 0]
    bank_state = [None] * 7
    aux_banks = [5, 6]
    aux_i = [0]

    def wq_dma(src_ap):
        q = wq_idx[0]
        wq_idx[0] += 1
        slot = q % 3
        dst = WQ[:, slot * 2048: slot * 2048 + src_ap.shape[-1]]
        waits = []
        if wq_slot_pe[slot] > 0:
            waits.append((sPE, wq_slot_pe[slot]))
        S.emit("gpsimd", lambda g, d=dst, s=src_ap: g.dma_start(d, s),
               waits=waits, inc=sWDs[slot], inc_n=16)
        return dst, (sWDs[slot], sWDs[slot].cnt), slot

    sAD2 = S.sem("sAD2")

    def scal_dma(fn, waits=()):
        S.emit("scalar", fn, waits=waits, inc=sAD2, inc_n=16)
        v = sAD2.cnt
        S.emit("scalar", lambda sc: sc.nop(), waits=[(sAD2, v)])
        return v

    def sync_dma(fn, waits=()):
        S.emit("sync", fn, waits=waits, inc=sAD, inc_n=16)
        v = sAD.cnt
        S.emit("sync", lambda s: s.nop(), waits=[(sAD, v)])
        return v

    def bank_wait(bank):
        st = bank_state[bank]
        return [st] if st is not None else []

    def bank_done(bank, sem, val):
        bank_state[bank] = (sem, val)

    def aux_next():
        b = aux_banks[aux_i[0] % len(aux_banks)]
        aux_i[0] += 1
        return b

    v_hmsk = sync_dma(lambda s: s.dma_start(HMSK[:, :], dt["hmask"][:, :]))
    v_vecs = sync_dma(lambda s: s.dma_start(
        B2f[0:16, 0:1280].rearrange("p (j c) -> p j c", c=128),
        dt["vecs"][:, :].rearrange("j (p c) -> p j c", p=16)))
    v_crow = sync_dma(lambda s: s.dma_start(B2f[0:16, 1280:1408],
                                            dt["crow"][:, :]))
    v_x0 = sync_dma(lambda s: s.dma_start(
        B1f[:, :].rearrange("p (f t) -> p f t", t=W),
        dt["x0"][:, :].rearrange("(f p) t -> p f t", p=128)))
    S.emit("gpsimd", lambda g: g.dma_start(WRT[:, :], dt["wrouter"][:, :]),
           inc=sWR, inc_n=16)
    v_wrt = sWR.cnt

    S.emit("gpsimd", lambda g: g.iota(B2f[:, 1408:1536], [[1, 128]],
                                      channel_multiplier=-1,
                                      allow_small_or_imprecise_dtypes=True),
           waits=[(sAD, v_vecs)], inc=sGP)
    S.emit("gpsimd", lambda g: g.iota(R16[:, :], [[0, 128]],
                                      channel_multiplier=1,
                                      allow_small_or_imprecise_dtypes=True),
           inc=sGP)
    v_iota = sGP.cnt
    S.emit("vector", lambda v: v.tensor_scalar(IDF[:, :], B2f[:, 1408:1536],
                                               0.0, None, op0=ALU.is_equal),
           waits=[(sGP, v_iota)], inc=sDV)
    S.emit("vector", lambda v: v.tensor_copy(IDB[:, :], IDF[:, :]), inc=sDV)
    S.emit("vector", lambda v: v.memset(ONESF[:, :], 1.0), inc=sDV)
    S.emit("vector", lambda v: v.memset(ONESB[:, :], 1.0), inc=sDV)
    S.emit("vector", lambda v: v.memset(ONESFB[:, :], 1.0), inc=sDV)
    S.emit("vector", lambda v: v.memset(S0[:, 0:1], 0.0), inc=sDV)
    S.emit("vector", lambda v: v.memset(S0B[:, 0:1], 0.0), inc=sDV)
    S.emit("vector", lambda v: v.memset(EPSC[:, :], EPS), inc=sDV)
    S.emit("vector", lambda v: v.memset(NEGC[:, :], -1.0e38), inc=sDV)
    v_consts = sDV.cnt

    SSTG = SSTGT[0:16, :]
    S.emit("scalar", lambda sc: sc.activation(SSTG, B2f[0:16, 1280:1408],
                                              AF.Silu, bias=0.0, scale=1.0),
           waits=[(sAD, v_crow)], inc=sAC)
    v_sstg = sAC.cnt
    pe_v = S.emit("tensor", lambda t: t.transpose(p7b[:, 0:16], SSTG, IDB[0:16, 0:16]),
                  waits=[(sAC, v_sstg), (sDV, v_consts)], inc=sPE)
    v_ssb = S.emit("scalar", lambda sc: sc.copy(SSB[:, :], p7b[:, 0:16]),
                   waits=[(sPE, pe_v)], inc=sAC)

    VEC_ORDER = ["mk", "mv", "mr", "cmk", "cmr", "g1", "b1", "g2", "b2", "tmf"]
    for j, nm in enumerate(VEC_ORDER):
        bk = aux_next()
        wl = [(sAD, v_vecs), (sDV, v_consts)] + bank_wait(bk)
        pe_v = S.emit("tensor",
                      lambda t, j_=j, b=bk: t.transpose(
                          psf[b][0:128, 0:16], B2f[0:16, j_ * 128:(j_ + 1) * 128],
                          IDF[0:16, 0:16]),
                      waits=wl, inc=sPE)
        ac_v = S.emit("scalar",
                      lambda sc, nm_=nm, b=bk: sc.copy(cvall(nm_), psf[b][0:128, 0:16]),
                      waits=[(sPE, pe_v)], inc=sAC)
        bank_done(bk, sAC, ac_v)

    sWM = [S.sem("sWM0"), S.sem("sWM1")]
    wm_half_pe = [0, 0]
    lna_sum_pe = [0, 0, 0]
    lna_bank_first = [True, True, True]
    pend_tq = []

    def _lna_sums(fc):
        for nb in range(3):
            nbw = [342, 342, 342][nb]
            w0 = nb * 342
            bk = 4 + nb
            waits = [(sAD, v_x0)]
            if lna_bank_first[nb]:
                waits += bank_wait(bk)
            r = S.emit(
                "tensor",
                lambda t, b=bk, st=lna_bank_first[nb], sp=(fc == FC - 1),
                f=fc, w0_=w0, n_=nbw:
                t.matmul(psf[b][0:1, 0:n_], ONESFB[:, :],
                         B1f[:, f * W + w0_: f * W + w0_ + n_],
                         start=st, stop=sp),
                waits=waits, inc=(sPE if fc == FC - 1 else None))
            lna_bank_first[nb] = False
            if r is not None:
                lna_sum_pe[nb] = r

    for g in range(24):
        half = g % 2
        hbase = half * 8192
        wl_d = [(sPE, wm_half_pe[half])] if wm_half_pe[half] else []
        S.emit("gpsimd", lambda gg, h=hbase, g_=g: gg.dma_start(
            B3b[:, h:h + 8192].rearrange("p (q c) -> p q c", c=2048),
            dt["wmod"][g_ * 4:(g_ + 1) * 4, :, :].transpose([1, 0, 2])),
            waits=wl_d, inc=sWM[half], inc_n=16)
        v_rdy = sWM[half].cnt
        bk = g % 2
        wl = [(sWM[half], v_rdy), (sAC, v_ssb)] + bank_wait(bk)
        pe_v = None
        for k in range(FC):
            r = S.emit("tensor",
                       lambda t, h=hbase, k_=k, b=bk, st=(k == 0),
                       sp=(k == FC - 1):
                       t.matmul(psf[b][0:1, 0:512],
                                SSB[:, k_:k_ + 1],
                                B3b[:, h:h + 8192]
                                .rearrange("p (q c) -> p q c", c=2048)
                                [:, :, k_ * 128:(k_ + 1) * 128],
                                start=st, stop=sp),
                       waits=(wl if k == 0 else []),
                       inc=(sPE if k == FC - 1 else None))
            pe_v = r if r is not None else pe_v
        wm_half_pe[half] = pe_v
        st0 = (g % 2) * 512
        ac_row = S.emit("scalar", lambda sc, b=bk, s0=st0: sc.copy(
            S1[0:1, s0:s0 + 512], psf[b][0:1, 0:512]),
            waits=[(sPE, pe_v)], inc=sAC)
        bank_done(bk, sAC, ac_row)
        pend_tq.append((g, st0, ac_row))
        if len(pend_tq) > 1 or g == 23:
            for (gq, s0q, acq) in pend_tq[:-1] if g != 23 else pend_tq:
                for tq in range(4):
                    Mb = gq * 4 + tq
                    bt = 2 + tq % 2
                    pe_t = S.emit("tensor",
                                  lambda t, b=bt, c=s0q + tq * 128: t.transpose(
                                      psf[b][:, 0:1], S1[0:1, c:c + 128],
                                      IDF[0:1, 0:1]),
                                  waits=[(sAC, acq)] + bank_wait(bt), inc=sPE)
                    ac_v = S.emit("scalar", lambda sc, b=bt, m=Mb: sc.copy(
                        MODV[:, m:m + 1], psf[b][:, 0:1]),
                        waits=[(sPE, pe_t)], inc=sAC)
                    bank_done(bt, sAC, ac_v)
            pend_tq[:] = (pend_tq[-1:] if g != 23 else [])
        if 2 <= g <= 17:
            _lna_sums(g - 2)
    for bk in (4, 5, 6):
        bank_done(bk, sPE, lna_sum_pe[bk - 4])
    v_mod = sAC.cnt

    S.emit("vector", lambda v: v.tensor_scalar(cvall("ca1"), MODV[:, 16:32],
                                               1.0, None, op0=ALU.add),
           waits=[(sAC, v_mod)], inc=sDV)
    S.emit("vector", lambda v: v.tensor_scalar(cvall("cb1"), MODV[:, 64:80],
                                               1.0, None, op0=ALU.add), inc=sDV)
    v_modpost = sDV.cnt
    for src0, dst in ((32, "gac"), (80, "gbc")):
        g = MODV[:, src0:src0 + 16]
        S.emit("vector", lambda v, g_=g: v.tensor_scalar(
            PTX, g_, 0.25, None, op0=ALU.max), inc=sDV)
        S.emit("vector", lambda v, g_=g: v.tensor_scalar(
            AUX[:, 0:16], g_, -0.25, None, op0=ALU.min), inc=sDV)
        S.emit("vector", lambda v, g_=g: v.tensor_scalar(
            AUX[:, 16:32], g_, 0.0, None, op0=ALU.is_ge), inc=sDV)
        S.emit("vector", lambda v: v.tensor_tensor(
            PTX, PTX, AUX[:, 0:16], op=ALU.subtract), inc=sDV)
        S.emit("vector", lambda v: v.tensor_tensor(
            PTX, PTX, AUX[:, 16:32], op=ALU.mult), inc=sDV)
        v_modpost = S.emit("vector", lambda v, d=dst: v.tensor_tensor(
            cvall(d), AUX[:, 0:16], PTX, op=ALU.add), inc=sDV)

    stats_sq_guard = [0]

    def emit_stats_phase(src_chunk_fn, scr_big_bf, width, nblocks, src_waits,
                         src_waits_fn=None, slot_guard=True, presums=None):
        del scr_big_bf
        del nblocks
        nbws = [342, 342, width - 684]
        nblocks = 3

        def w0_of(nb):
            return sum(nbws[:nb])

        if presums is not None:
            sum_pe = list(presums)
        else:
            sum_pe = [0] * nblocks
        for nb in (() if presums is not None else range(nblocks)):
            nbw = nbws[nb]
            w0 = w0_of(nb)
            bk = 4 + nb
            wl0 = bank_wait(bk)
            for fc in range(FC):
                wls = src_waits_fn(fc) if src_waits_fn else src_waits
                waits = (wl0 + wls) if fc == 0 else wls
                r = S.emit(
                    "tensor",
                    lambda t, a=src_chunk_fn(fc), b=bk, st=(fc == 0),
                    sp=(fc == FC - 1), w0_=w0, n_=nbw:
                    t.matmul(psf[b][0:1, 0:n_], ONESFB[:, :],
                             a[:, w0_:w0_ + n_], start=st, stop=sp),
                    waits=waits, inc=(sPE if fc == FC - 1 else None))
                sum_pe[nb] = r if r is not None else sum_pe[nb]
        slot_mark = [stats_sq_guard[0], stats_sq_guard[0]]
        sq_pe = [0] * nblocks
        for fc in range(FC):
            slot = WKVT[:, 0:width] if fc % 2 == 0 else S0[:, 1:1 + width]
            wls = src_waits_fn(fc) if src_waits_fn else src_waits
            wl = list(wls) + [(sPE, slot_mark[fc % 2])]
            if fc < 2 and slot_guard:
                wl += list(src_waits)
            a1 = S.emit("scalar", lambda sc, s=src_chunk_fn(fc), d=slot:
                        sc.activation(d, s, AF.Square, bias=0.0, scale=1.0),
                        waits=wl, inc=sAC)
            last = 0
            for nb in range(nblocks):
                nbw = nbws[nb]
                w0 = w0_of(nb)
                bk = nb
                waits = [(sAC, a1)] + (bank_wait(bk) if fc == 0 else [])
                r = S.emit(
                    "tensor",
                    lambda t, d=slot, b=bk, st=(fc == 0), sp=(fc == FC - 1),
                    w0_=w0, n_=nbw:
                    t.matmul(psf[b][0:1, 0:n_], ONESFB[:, :],
                             d[:, w0_:w0_ + n_], start=st, stop=sp),
                    waits=waits,
                    inc=(sPE if nb == nblocks - 1 or fc == FC - 1 else None))
                if r is not None:
                    last = r
                    sq_pe[nb] = max(sq_pe[nb], r)
            slot_mark[fc % 2] = last
        stats_sq_guard[0] = max(slot_mark)
        v_bc = 0
        prev_bc_pe = 0
        for nb in range(nblocks):
            nbw = nbws[nb]
            w0 = w0_of(nb)
            Cm = STATROW[:, 0:nbw]
            Bv = STATROW[:, 342:342 + nbw]
            Dr = STATROW[:, 684:684 + nbw]
            wl1 = [(sPE, sum_pe[nb])]
            if prev_bc_pe:
                wl1.append((sPE, prev_bc_pe))
            dv1 = S.emit("vector", lambda v, b=4 + nb, c_=Cm, n_=nbw:
                v.tensor_scalar(c_, psf[b][0:1, 0:n_], IH, None, op0=ALU.mult),
                waits=wl1, inc=sDV)
            dv2 = S.emit("vector", lambda v, b=nb, bv=Bv, n_=nbw:
                v.tensor_scalar(bv, psf[b][0:1, 0:n_], IH, None, op0=ALU.mult),
                waits=[(sPE, sq_pe[nb])], inc=sDV)
            S.emit("vector", lambda v, d=Dr, c_=Cm: v.tensor_tensor(
                d, c_, c_, op=ALU.mult), inc=sDV)
            dv4 = S.emit("vector", lambda v, bv=Bv, d=Dr: v.tensor_tensor(
                bv, bv, d, op=ALU.subtract), inc=sDV)
            S.emit("scalar", lambda sc, d=Dr, bv=Bv: sc.activation(
                d, bv, AF.Ln, bias=EPSC[:, :], scale=1.0),
                waits=[(sDV, dv4)], inc=sAC)
            ac2 = S.emit("scalar", lambda sc, d=Dr: sc.activation(
                d, d, AF.Exp, bias=0.0, scale=-0.5), inc=sAC)
            dv5 = S.emit("vector", lambda v, c_=Cm, d=Dr: v.tensor_tensor(
                c_, c_, d, op=ALU.mult), waits=[(sAC, ac2)], inc=sDV)
            bank_done(4 + nb, sDV, dv1)
            bank_done(nb, sDV, dv2)
            for which, srcslot in ((0, Dr), (1, Cm)):
                bkb = 3
                wl = bank_wait(bkb) + [(sDV, dv5)]
                pe_v = S.emit("tensor", lambda t, b=bkb, s=srcslot, n_=nbw:
                              t.matmul(psf[b][:, 0:n_], ONESF[0:1, :], s,
                                       start=True, stop=True), waits=wl, inc=sPE)
                prev_bc_pe = pe_v
                v_bc = S.emit("scalar", lambda sc, b=bkb, wh=which, w0_=w0, n_=nbw:
                              sc.copy(BCB[:, wh * W + w0_: wh * W + w0_ + n_],
                                      psf[b][:, 0:n_]),
                              waits=[(sPE, pe_v)], inc=sAC)
                bank_done(bkb, sAC, v_bc)
        return v_bc

    v_lna = emit_stats_phase(lambda fc: B1f[:, fc * W: fc * W + W],
                             B2b, W, 3, [(sAD, v_x0)], presums=lna_sum_pe)

    v_h = 0
    h_marks = []
    for fc in range(FC):
        src = B1f[:, fc * W: fc * W + W]
        dst = B2f[:, fc * W: fc * W + W]
        S.emit("vector", lambda v, s=src: v.tensor_tensor(
            DT[:, 0:W], s, BCB[:, 0:W], op=ALU.mult),
            waits=[(sAC, v_lna), (sDV, v_modpost)], inc=sDV)
        S.emit("vector", lambda v: v.tensor_tensor(
            DT[:, 0:W], DT[:, 0:W], BCB[:, W:2 * W], op=ALU.subtract), inc=sDV)
        v_h = S.emit("vector", lambda v, d=dst, f=fc: v.tensor_scalar(
            d, DT[:, 0:W], cv("ca1", f), MODV[:, f:f + 1],
            op0=ALU.mult, op1=ALU.add), inc=sDV)
        h_marks.append(v_h)

    v_ln1 = emit_stats_phase(lambda fc: B2f[:, fc * W: fc * W + W],
                             B1b, W, 3, [(sDV, v_h)],
                             src_waits_fn=lambda fc: [(sDV, h_marks[fc])],
                             slot_guard=False)

    def mixA(fc, width=W):
        return B1b[:, fc * W: fc * W + width]

    def mixB(fc, width=W):
        return B1b[:, FCW + fc * W: FCW + fc * W + width]

    def emit_mix_chunk(fc, v_bcast, src_big, gslot, bslot, mixes, width=W,
                       extra_waits=()):
        last = 0
        if True:
            src = src_big[:, fc * W: fc * W + W]
            wl0 = [(sAC, v_bcast), (sAD, v_hmsk)] + list(extra_waits)
            S.emit("vector", lambda v, s=src: v.tensor_tensor(
                S0B[:, 1:W + 1], s, BCB[:, 0:W], op=ALU.mult),
                waits=wl0, inc=sDV)
            S.emit("vector", lambda v: v.tensor_tensor(
                S0B[:, 1:W + 1], S0B[:, 1:W + 1], BCB[:, W:2 * W],
                op=ALU.subtract), inc=sDV)
            S.emit("vector", lambda v, f=fc, g=gslot, b_=bslot: v.tensor_scalar(
                S0B[:, 1:W + 1], S0B[:, 1:W + 1], cv(g, f), cv(b_, f),
                op0=ALU.mult, op1=ALU.add), inc=sDV)
            S.emit("vector", lambda v: v.tensor_scalar(
                S0B[:, 2:3], S0B[:, 2:3], HMSK[:, 0:1], None, op0=ALU.mult),
                inc=sDV)
            S.emit("vector", lambda v: v.tensor_tensor(
                DT[:, 0:W], S0B[:, 1:W + 1], S0B[:, 0:W], op=ALU.subtract),
                inc=sDV)
            for slotname, dest_fn in mixes:
                if width == W:
                    last = S.emit("vector", lambda v, f=fc, sn=slotname, df=dest_fn:
                                  v.scalar_tensor_tensor(
                                      df(f), DT[:, 0:W], cv(sn, f), S0B[:, 0:W],
                                      op0=ALU.mult, op1=ALU.add), inc=sDV)
                else:
                    last = S.emit("vector", lambda v, f=fc, sn=slotname, df=dest_fn:
                                  v.scalar_tensor_tensor(
                                      df(f), DT[:, 2:2 + NT], cv(sn, f),
                                      S0B[:, 2:2 + NT], op0=ALU.mult, op1=ALU.add),
                                  inc=sDV)
        return last

    def emit_mix_pair(v_bcast, src_big, gslot, bslot, mixes, width=W,
                      extra_waits=()):
        last = 0
        for fc in range(FC):
            last = emit_mix_chunk(fc, v_bcast, src_big, gslot, bslot, mixes,
                                  width, extra_waits if fc == 0 else ())
        return last

    v_mix_rk = emit_mix_pair(v_ln1, B2f, "g1", "b1",
                             [("mr", lambda fc: mixA(fc)),
                              ("mk", lambda fc: mixB(fc))])

    def emit_hxh(wname, moving_fn, evac_fn, v_moving, nblk=3, width=W,
                 extra_mm_waits=(), after_mc=None):
        nbw = width // nblk
        for mc in range(FC):
            slot_ap, rdy, slot = wq_dma(dt[wname][mc, :, :])
            pe_last = None
            for nb in range(nblk):
                c0 = nb * nbw
                bank = (mc * nblk + nb) % 4
                wl = [rdy, (sDV, v_moving)] + bank_wait(bank) + \
                    list(extra_mm_waits)
                for k in range(FC):
                    r = S.emit(
                        "tensor",
                        lambda t, sl=slot_ap, k_=k, b=bank, c=c0,
                        st=(k == 0), sp=(k == FC - 1), mf=moving_fn:
                        t.matmul(psf[b][:, 0:nbw], sl[:, k_ * 128:(k_ + 1) * 128],
                                 mf(k_, c, nbw), start=st, stop=sp),
                        waits=(wl if k == 0 else []),
                        inc=(sPE if k == FC - 1 else None))
                    pe_last = r if r is not None else pe_last
                sem, val = evac_fn(mc, nb, c0, nbw, psf[bank], [(sPE, pe_last)])
                bank_done(bank, sem, val)
            wq_slot_pe[slot] = pe_last
            if after_mc is not None:
                after_mc(mc)

    def evac_sigr(mc, nb, c0, cw, ps, waits):
        v = S.emit("scalar", lambda sc, m=mc, c=c0, w=cw, p=ps:
                   sc.activation(B3b[:, m * W + c: m * W + c + w], p[:, 0:w],
                                 AF.Sigmoid, bias=0.0, scale=1.0),
                   waits=waits, inc=sAC)
        return sAC, v

    emit_hxh("wr", lambda k, c0, nw: mixA(k)[:, c0:c0 + nw], evac_sigr, v_mix_rk)
    v_sigr = sAC.cnt
    v_r_pe = sPE.cnt

    def evac_wkv(mc, nb, c0, cw, ps, waits):
        a1 = S.emit("scalar", lambda sc, p=ps, m=mc, w=cw:
                    sc.activation(WKVT[:, 0:w], p[:, 0:w], AF.Identity,
                                  bias=cv("tmf", m), scale=1.0),
                    waits=waits + [(sDV, sDV.cnt), (sPE, sPE.cnt)], inc=sAC)
        S.emit("scalar", lambda sc, w=cw: sc.activation(
            WKVT[:, 342:342 + w], WKVT[:, 0:w], AF.Relu, bias=NEGC[:, :],
            scale=-1.0), inc=sAC)
        S.emit("scalar", lambda sc, w=cw: sc.activation(
            GATEB[:, 0:w], WKVT[:, 342:342 + w], AF.Exp, bias=0.0, scale=-1.0),
            inc=sAC)
        a3 = S.emit("scalar", lambda sc, w=cw: sc.activation(
            GATEB[:, 342:342 + w], WKVT[:, 342:342 + w], AF.Exp, bias=0.0,
            scale=1.0), inc=sAC)
        S.emit("vector", lambda v, w=cw: v.tensor_tensor(
            GATEB[:, 0:w], GATEB[:, 0:w], GATEB[:, 342:342 + w], op=ALU.mult),
            waits=[(sAC, a3)], inc=sDV)
        d4 = S.emit("vector", lambda v, m=mc, c=c0, w=cw: v.tensor_tensor(
            B3b[:, m * W + c: m * W + c + w], B3b[:, m * W + c: m * W + c + w],
            GATEB[:, 0:w], op=ALU.mult), inc=sDV)
        return sDV, d4

    mixv_last = [0]

    def _mixv_chunk(mc):
        mixv_last[0] = emit_mix_chunk(
            mc, v_ln1, B2f, "g1", "b1", [("mv", lambda fc: mixA(fc))],
            extra_waits=([(sPE, v_r_pe)] if mc == 0 else ()))

    emit_hxh("wk", lambda k, c0, nw: mixB(k)[:, c0:c0 + nw], evac_wkv, v_mix_rk,
             extra_mm_waits=[(sAC, v_sigr)], after_mc=_mixv_chunk)
    v_gate = sDV.cnt
    v_mix_v = mixv_last[0]

    def evac_gin(mc, nb, c0, cw, ps, waits):
        v = S.emit("vector", lambda vv, m=mc, c=c0, w=cw, p=ps:
                   vv.tensor_tensor(mixB(m)[:, c: c + w], p[:, 0:w],
                                    B3b[:, m * W + c: m * W + c + w], op=ALU.mult),
                   waits=waits + [(sDV, v_gate)], inc=sDV)
        return sDV, v

    emit_hxh("wv", lambda k, c0, nw: mixA(k)[:, c0:c0 + nw], evac_gin, v_mix_v)
    v_gin = sDV.cnt

    xa_marks = [0] * FC

    def evac_xa(mc, nb, c0, cw, ps, waits):
        v = S.emit("vector", lambda vv, m=mc, c=c0, w=cw, p=ps:
                   vv.tensor_tensor(B2f[:, m * W + c: m * W + c + w],
                                    B2f[:, m * W + c: m * W + c + w],
                                    p[:, 0:w], op=ALU.add), waits=waits, inc=sDV)
        xa_marks[mc] = v
        return sDV, v

    emit_hxh("wo", lambda k, c0, nw: mixB(k)[:, c0:c0 + nw], evac_xa, v_gin)
    v_xa = sDV.cnt

    v_ln2 = emit_stats_phase(lambda fc: B2f[:, fc * W: fc * W + W],
                             B1b, W, 3, [(sDV, v_xa)],
                             src_waits_fn=lambda fc: [(sDV, xa_marks[fc])],
                             slot_guard=False)

    v_mix_ck = emit_mix_pair(v_ln2, B2f, "g2", "b2",
                             [("cmr", lambda fc: mixA(fc, NT)),
                              ("cmk", lambda fc: mixB(fc, NT))], width=NT)

    def evac_scr(mc, nb, c0, cw, ps, waits):
        v = S.emit("scalar", lambda sc, m=mc, c=c0, w=cw, p=ps:
                   sc.activation(B3b[:, m * NT + c: m * NT + c + w], p[:, 0:w],
                                 AF.Sigmoid, bias=0.0, scale=1.0),
                   waits=waits, inc=sAC)
        return sAC, v

    emit_hxh("wcr", lambda k, c0, nw: mixA(k, NT)[:, c0:c0 + nw], evac_scr,
             v_mix_ck, nblk=2, width=NT, extra_mm_waits=[(sDV, v_gin)])
    v_scr = sAC.cnt

    v_xasp = sync_dma(lambda s: s.dma_start(
        dt["xasp"][:, :].rearrange("(f p) t -> p f t", p=128),
        B2f[:, :].rearrange("p (f t) -> p f t", t=W)[:, :, 2:W]),
        waits=[(sDV, v_mix_ck)])

    v_xb = [0, 0]
    v_xbsp = 0
    xb1_marks = []
    for tb in range(2):
        c0tb = tb * TB
        wl = [(sAD, v_xasp), (sAC, v_scr)]
        if tb == 1:
            wl.append((sAD, v_xbsp))
        v_xare = sync_dma(lambda s, c=c0tb: s.dma_start(
            B1f[:, 0:FC * TB].rearrange("p (f t) -> p f t", t=TB),
            dt["xasp"][:, :].rearrange("(f p) t -> p f t", p=128)[:, :, c:c + TB]),
            waits=wl)

        v_skk = 0
        for mg in range(DC4):
            slot_ap, rdy, slot = wq_dma(dt["wkey"][mg, :, :])
            bank = mg % 4
            wl2 = [rdy, (sDV, v_mix_ck), (sAD, v_xasp)] + bank_wait(bank)
            pe_last = None
            for k in range(FC):
                r = S.emit(
                    "tensor",
                    lambda t, sl=slot_ap, k_=k, b=bank, c=c0tb,
                    st=(k == 0), sp=(k == FC - 1):
                    t.matmul(psf[b][:, 0:TB], sl[:, k_ * 128:(k_ + 1) * 128],
                             mixB(k_, NT)[:, c:c + TB], start=st, stop=sp),
                    waits=(wl2 if k == 0 else []),
                    inc=(sPE if k == FC - 1 else None))
                pe_last = r if r is not None else pe_last
            wq_slot_pe[slot] = pe_last
            v_skk = S.emit("scalar", lambda sc, b=bank, m=mg:
                           sc.activation(B2b[:, m * TB:(m + 1) * TB],
                                         psf[b][:, 0:TB], AF.Silu,
                                         bias=0.0, scale=1.0),
                           waits=[(sPE, pe_last)], inc=sAC)
            bank_done(bank, sAC, v_skk)

        for m in range(FC):
            bank = m % 4
            first = True
            pe_last = None
            for kq in range(4):
                slot_ap, rdy, slot = wq_dma(dt["wval"][m, :, kq * 2048:(kq + 1) * 2048])
                wl3 = [rdy, (sAC, v_skk)] + (bank_wait(bank) if first else [])
                for kk in range(16):
                    g = kq * 16 + kk
                    r = S.emit(
                        "tensor",
                        lambda t, sl=slot_ap, k_=kk, g_=g, b=bank,
                        st=first, sp=(kq == 3 and kk == 15):
                        t.matmul(psf[b][:, 0:TB], sl[:, k_ * 128:(k_ + 1) * 128],
                                 B2b[:, g_ * TB:(g_ + 1) * TB], start=st, stop=sp),
                        waits=(wl3 if kk == 0 else []),
                        inc=(sPE if kk == 15 else None))
                    pe_last = r if r is not None else pe_last
                    first = False
                wq_slot_pe[slot] = pe_last
            S.emit("vector", lambda v, b=bank, m_=m, c=c0tb:
                   v.tensor_tensor(S1[:, 0:TB], psf[b][:, 0:TB],
                                   B3b[:, m_ * NT + c: m_ * NT + c + TB],
                                   op=ALU.mult),
                   waits=[(sPE, pe_last), (sAC, v_scr), (sAD, v_xare)], inc=sDV)
            dv = S.emit("vector", lambda v, m_=m:
                        v.tensor_tensor(B1f[:, m_ * TB:(m_ + 1) * TB], S1[:, 0:TB],
                                        B1f[:, m_ * TB:(m_ + 1) * TB], op=ALU.add),
                        inc=sDV)
            bank_done(bank, sDV, dv)
            if tb == 1:
                xb1_marks.append(dv)
        v_xb[tb] = sDV.cnt

        if tb == 0:
            v_xbsp = sync_dma(lambda s: s.dma_start(
                dt["xbsp"][:, :].rearrange("(f p) t -> p f t", p=128),
                B1f[:, 0:FC * TB].rearrange("p (f t) -> p f t", t=TB)),
                waits=[(sDV, v_xb[0])])

    v_xb1cp = 0
    for fc in range(FC):
        v_xb1cp = S.emit("vector", lambda v, f=fc: v.tensor_copy(
            B2f[:, f * W + 2 + TB: f * W + 2 + 2 * TB], B1f[:, f * TB:(f + 1) * TB]),
            waits=[(sDV, xb1_marks[fc])], inc=sDV)
    v_xbre = scal_dma(lambda s: s.dma_start(
        B2f[:, :].rearrange("p (f t) -> p f t", t=W)[:, :, 2:2 + TB],
        dt["xbsp"][:, :].rearrange("(f p) t -> p f t", p=128)),
        waits=[(sDV, v_xb1cp)])
    v_x0re = sync_dma(lambda s: s.dma_start(
        B1f[:, 0:FC * NT].rearrange("p (f t) -> p f t", t=NT),
        dt["x0"][:, :].rearrange("(f p) t -> p f t", p=128)[:, :, 2:W]),
        waits=[(sDV, v_xb1cp)])

    v_x1 = 0
    x1_marks = []
    for fc in range(FC):
        v_x1 = S.emit("vector", lambda v, f=fc: v.scalar_tensor_tensor(
            B2f[:, f * W + 2: f * W + 2 + NT],
            B2f[:, f * W + 2: f * W + 2 + NT], cv("gac", f),
            B1f[:, f * NT:(f + 1) * NT], op0=ALU.mult, op1=ALU.add),
            waits=[(sAD2, v_xbre), (sAD, v_x0re)], inc=sDV)
        x1_marks.append(v_x1)

    v_lna2 = emit_stats_phase(lambda fc: B2f[:, fc * W + 2: fc * W + 2 + NT],
                              B1b, NT, 2, [(sDV, v_x1)],
                              src_waits_fn=lambda fc: [(sDV, x1_marks[fc])],
                              slot_guard=False)

    v_hf = 0
    for fc in range(FC):
        src = B2f[:, fc * W + 2: fc * W + 2 + NT]
        S.emit("vector", lambda v, s=src: v.tensor_tensor(
            DT[:, 0:NT], s, BCB[:, 0:NT], op=ALU.mult),
            waits=[(sAC, v_lna2)], inc=sDV)
        S.emit("vector", lambda v: v.tensor_tensor(
            DT[:, 0:NT], DT[:, 0:NT], BCB[:, W:W + NT], op=ALU.subtract), inc=sDV)
        v_hf = S.emit("vector", lambda v, f=fc: v.tensor_scalar(
            B3b[:, f * NT:(f + 1) * NT], DT[:, 0:NT], cv("cb1", f),
            MODV[:, 48 + f:49 + f], op0=ALU.mult, op1=ALU.add), inc=sDV)

    v_logit = 0
    for nb in range(2):
        bk = aux_next()
        wl = [(sWR, v_wrt), (sDV, v_hf)] + bank_wait(bk)
        pe_last = None
        for k in range(FC):
            r = S.emit("tensor",
                       lambda t, k_=k, b=bk, c=nb * 512,
                       st=(k == 0), sp=(k == FC - 1):
                       t.matmul(psf[b][0:16, 0:512],
                                WRT[:, k_ * 16:(k_ + 1) * 16],
                                B3b[:, k_ * NT + c: k_ * NT + c + 512],
                                start=st, stop=sp),
                       waits=(wl if k == 0 else []),
                       inc=(sPE if k == FC - 1 else None))
            pe_last = r if r is not None else pe_last
        v_logit = S.emit("scalar", lambda sc, b=bk, c=nb * 512:
                         sc.copy(SMALL[0:16, c:c + 512], psf[b][0:16, 0:512]),
                         waits=[(sPE, pe_last)], inc=sAC)
        bank_done(bk, sAC, v_logit)

    v_gatefm = 0
    for t8 in range(8):
        bk = aux_next()
        wl = [(sAC, v_logit)] + bank_wait(bk)
        pe_v = S.emit("tensor", lambda t, b=bk, i=t8: t.transpose(
            psf[b][0:128, 0:16], SMALL[0:16, i * 128:(i + 1) * 128],
            IDF[0:16, 0:16]), waits=wl, inc=sPE)
        ac_v = S.emit("scalar", lambda sc, b=bk, i=t8: sc.copy(
            PT[:, i * 16:(i + 1) * 16], psf[b][0:128, 0:16]),
            waits=[(sPE, pe_v)], inc=sAC)
        bank_done(bk, sAC, ac_v)
        tile = PT[:, t8 * 16:(t8 + 1) * 16]
        gtile = tile
        c = t8
        S.emit("vector", lambda v, tl=tile, c_=c: v.reduce_max(
            AUX[:, c_:c_ + 1], tl, axis=AX.X), waits=[(sAC, ac_v)], inc=sDV)
        nmx = S.emit("vector", lambda v, c_=c: v.tensor_scalar(
            AUX[:, 8 + c_:9 + c_], AUX[:, c_:c_ + 1], -1.0, None, op0=ALU.mult),
            inc=sDV)
        a1 = S.emit("scalar", lambda sc, tl=tile, c_=c: sc.activation(
            tl, tl, AF.Exp, bias=AUX[:, 8 + c_:9 + c_], scale=1.0),
            waits=[(sDV, nmx)], inc=sAC)
        S.emit("vector", lambda v, tl=tile, c_=c: v.reduce_sum(
            AUX[:, 16 + c_:17 + c_], tl, axis=AX.X), waits=[(sAC, a1)], inc=sDV)
        S.emit("vector", lambda v, c_=c: v.reciprocal(
            AUX[:, 24 + c_:25 + c_], AUX[:, 16 + c_:17 + c_]), inc=sDV)
        S.emit("vector", lambda v, tl=tile, c_=c: v.tensor_scalar(
            tl, tl, AUX[:, 24 + c_:25 + c_], None, op0=ALU.mult), inc=sDV)
        S.emit("vector", lambda v, tl=tile, c_=c: v.reduce_max(
            AUX[:, 32 + c_:33 + c_], tl, axis=AX.X), inc=sDV)
        S.emit("vector", lambda v, tl=tile, c_=c: v.tensor_scalar(
            PTX, tl, AUX[:, 32 + c_:33 + c_], None, op0=ALU.is_lt),
            inc=sDV)
        S.emit("vector", lambda v, tl=tile: v.tensor_tensor(
            PTX, tl, PTX, op=ALU.mult), inc=sDV)
        S.emit("vector", lambda v, c_=c: v.reduce_max(
            AUX[:, 40 + c_:41 + c_], PTX, axis=AX.X), inc=sDV)
        S.emit("vector", lambda v, tl=tile, c_=c: v.tensor_scalar(
            PTX, tl, AUX[:, 40 + c_:41 + c_], None, op0=ALU.is_ge),
            inc=sDV)
        dvg = S.emit("vector", lambda v, tl=tile, g=gtile: v.tensor_tensor(
            g, tl, PTX, op=ALU.mult), inc=sDV)
        bk2 = aux_next()
        wl2 = [(sDV, dvg)] + bank_wait(bk2)
        pe2 = S.emit("tensor", lambda t, b=bk2, g=gtile: t.transpose(
            psf[b][0:16, 0:128], g, IDF[:, :]), waits=wl2, inc=sPE)
        v_gatefm = S.emit("scalar", lambda sc, b=bk2, i=t8: sc.copy(
            SMALL[0:16, i * 128:(i + 1) * 128], psf[b][0:16, 0:128]),
            waits=[(sPE, pe2)], inc=sAC)
        bank_done(bk2, sAC, v_gatefm)

    v_x1sp = sync_dma(lambda s: s.dma_start(
        dt["x1sp"][:, :].rearrange("(f p) t -> p f t", p=128),
        B2f[:, :].rearrange("p (f t) -> p f t", t=W)[:, :, 2:W]),
        waits=[(sDV, v_x1)])

    GROUPS = [[("sh", 0), ("sh", 1), ("rt", 0), ("rt", 1)],
              [("rt", 2), ("rt", 3), ("rt", 4), ("rt", 5)],
              [("rt", 6), ("rt", 7), ("rt", 8), ("rt", 9)],
              [("rt", 10), ("rt", 11), ("rt", 12), ("rt", 13)],
              [("rt", 14), ("rt", 15)]]

    def mid_ap(q, dc, c0, cw):
        base = q * (DCE * NT) + dc * NT + c0
        return B1b[:, base: base + cw]

    aux_banks[:] = [6]
    racc_started = False
    gateb_state = [None]
    selw_pe = [0]
    first_up = [True]

    for gi, grp in enumerate(GROUPS):
        gb_ready = {}
        grp_mid_sem = None
        grp_mid_val = 0
        for qi, (kind, e) in enumerate(grp):
            if kind == "rt":
                slot_g = 0
                wl0 = [(sGP, v_iota)]
                if selw_pe[0]:
                    wl0.append((sPE, selw_pe[0]))
                dv_sel = S.emit("vector", lambda v, e_=e: v.tensor_scalar(
                    SELW[:, :], R16[:, :], float(e_), None, op0=ALU.is_equal),
                    waits=wl0, inc=sDV)
                for nb in range(2):
                    bk = aux_next()
                    wl = [(sAC, v_gatefm), (sDV, dv_sel)] + bank_wait(bk)
                    if gateb_state[slot_g] is not None:
                        wl.append(gateb_state[slot_g])
                    pe_v = S.emit("tensor", lambda t, b=bk, c=nb * 512:
                                  t.matmul(psf[b][:, 0:512], SELW[:, :],
                                           SMALL[0:16, c:c + 512],
                                           start=True, stop=True), waits=wl, inc=sPE)
                    selw_pe[0] = pe_v
                    ac_v = S.emit("scalar", lambda sc, b=bk, c=nb * 512:
                                  sc.copy(GATEB[:, c: c + 512],
                                          psf[b][:, 0:512]),
                                  waits=[(sPE, pe_v)], inc=sAC)
                    bank_done(bk, sAC, ac_v)
                gb_ready[e] = sAC.cnt
            wnm = "wshu" if kind == "sh" else "wrtu"
            exp_last = None
            for dc in range(DCE):
                slot_ap, rdy, slot = wq_dma(dt[wnm][e * DCE + dc, :, :])
                for nb in range(2):
                    c0 = nb * 512
                    bank = (qi * DCE * 2 + dc * 2 + nb) % 4
                    wl = [rdy, (sDV, v_hf)] + bank_wait(bank)
                    if first_up[0]:
                        wl += [(sDV, v_x1), (sAC, v_lna2)]
                        first_up[0] = False
                    pe_last = None
                    for k in range(FC):
                        r = S.emit(
                            "tensor",
                            lambda t, sl=slot_ap, k_=k, b=bank, c=c0,
                            st=(k == 0), sp=(k == FC - 1):
                            t.matmul(psf[b][:, 0:512],
                                     sl[:, k_ * 128:(k_ + 1) * 128],
                                     B3b[:, k_ * NT + c: k_ * NT + c + 512],
                                     start=st, stop=sp),
                            waits=(wl if k == 0 else []),
                            inc=(sPE if k == FC - 1 else None))
                        pe_last = r if r is not None else pe_last
                    ac_v = S.emit("scalar", lambda sc, b=bank, q=qi, d=dc, c=c0:
                                  sc.activation(mid_ap(q, d, c, 512),
                                                psf[b][:, 0:512], AF.Gelu,
                                                bias=0.0, scale=1.0),
                                  waits=[(sPE, pe_last)], inc=sAC)
                    if kind == "rt":
                        dv_v = S.emit("vector",
                                      lambda v, q=qi, d=dc, c=c0:
                                      v.tensor_tensor(
                                          mid_ap(q, d, c, 512),
                                          mid_ap(q, d, c, 512),
                                          GATEB[:, c: c + 512],
                                          op=ALU.mult),
                                      waits=[(sAC, ac_v), (sAC, gb_ready[e])],
                                      inc=sDV)
                        bank_done(bank, sDV, dv_v)
                        grp_mid_sem, grp_mid_val = sDV, dv_v
                        exp_last = (sDV, dv_v)
                    else:
                        bank_done(bank, sAC, ac_v)
                        grp_mid_sem, grp_mid_val = sAC, ac_v
                wq_slot_pe[slot] = sPE.cnt
            if kind == "rt":
                gateb_state[0] = exp_last
        mid_wait = (grp_mid_sem, grp_mid_val)

        for mc in range(FC):
            first = [True, True]
            pe_last2 = [None, None]
            for qi, (kind, e) in enumerate(grp):
                wnm = "wshd" if kind == "sh" else "wrtd"
                slot_ap, rdy, slot = wq_dma(dt[wnm][e * FC + mc, :, :])
                for nb in range(2):
                    bank = 4 + nb
                    c0 = nb * 512
                    wl = [rdy, mid_wait]
                    if first[nb]:
                        wl += bank_wait(bank)
                    for k in range(DCE):
                        r = S.emit(
                            "tensor",
                            lambda t, sl=slot_ap, k_=k, b=bank, q=qi, c=c0,
                            st=first[nb],
                            sp=(qi == len(grp) - 1 and k == DCE - 1):
                            t.matmul(psf[b][:, 0:512],
                                     sl[:, k_ * 128:(k_ + 1) * 128],
                                     mid_ap(q, k_, c, 512), start=st, stop=sp),
                            waits=(wl if k == 0 else []),
                            inc=(sPE if k == DCE - 1 else None))
                        pe_last2[nb] = r if r is not None else pe_last2[nb]
                        first[nb] = False
                wq_slot_pe[slot] = sPE.cnt
            for nb in range(2):
                bank = 4 + nb
                c0 = nb * 512
                dst = B2f[:, mc * NT + c0: mc * NT + c0 + 512]
                if not racc_started:
                    wl = [(sPE, pe_last2[nb]), (sAD, v_x1sp), (sDV, v_hf),
                          (sAC, v_lna2)]
                    dv = S.emit("vector", lambda v, b=bank, d=dst:
                                v.tensor_copy(d, psf[b][:, 0:512]),
                                waits=wl, inc=sDV)
                else:
                    dv = S.emit("vector", lambda v, b=bank, d=dst:
                                v.tensor_tensor(d, d, psf[b][:, 0:512],
                                                op=ALU.add),
                                waits=[(sPE, pe_last2[nb])], inc=sDV)
                bank_done(bank, sDV, dv)
        racc_started = True
    v_racc = sDV.cnt

    v_x1re = sync_dma(lambda s: s.dma_start(
        B1f[:, 0:FC * NT].rearrange("p (f t) -> p f t", t=NT),
        dt["x1sp"][:, :].rearrange("(f p) t -> p f t", p=128)),
        waits=[(sDV, v_racc)])

    v_fin = 0
    for fc in range(FC):
        v_fin = S.emit("vector", lambda v, f=fc: v.scalar_tensor_tensor(
            B2f[:, f * NT:(f + 1) * NT], B2f[:, f * NT:(f + 1) * NT],
            cv("gbc", f), B1f[:, f * NT:(f + 1) * NT],
            op0=ALU.mult, op1=ALU.add),
            waits=[(sAD, v_x1re), (sDV, v_racc)], inc=sDV)
    sync_dma(lambda s: s.dma_start(
        dt["out"][:, :].rearrange("(f p) t -> p f t", p=128),
        B2f[:, 0:FC * NT].rearrange("p (f t) -> p f t", t=NT)),
        waits=[(sDV, v_fin)])

    with ctx:
        handles = {}
        for s in S.sems:
            handles[s.name] = ctx.enter_context(nc.semaphore(s.name))
        with nc.Block() as block:
            own_sem = {"vector": sDV, "scalar": sAC, "gpsimd": sGP}

            def make_section(eng_name):
                own = own_sem.get(eng_name)

                def run(eng):
                    own_cnt = 0
                    for waits, fn, inc, inc_n in S.steps[eng_name]:
                        if own is not None and own_cnt > 0:
                            eng.wait_ge(handles[own.name], own_cnt)
                        for sm, val in waits:
                            eng.wait_ge(handles[sm.name], val)
                        inst = fn(eng)
                        if inc is not None:
                            inst.then_inc(handles[inc.name], inc_n)
                            if inc is own:
                                own_cnt += inc_n
                return run
            block.tensor(make_section("tensor"))
            block.vector(make_section("vector"))
            block.scalar(make_section("scalar"))
            block.sync(make_section("sync"))
            block.gpsimd(make_section("gpsimd"))
    return nc


def _prep(inputs):
    f32 = np.float32
    x = np.asarray(inputs["L0_attn_x"], f32)
    vecs = np.stack([
        np.asarray(inputs["L0_attn_tm_mk"], f32).reshape(H),
        np.asarray(inputs["L0_attn_tm_mv"], f32).reshape(H),
        np.asarray(inputs["L0_attn_tm_mr"], f32).reshape(H),
        np.asarray(inputs["L0_attn_cm_mk"], f32).reshape(H),
        np.asarray(inputs["L0_attn_cm_mr"], f32).reshape(H),
        np.asarray(inputs["L0_attn_ln1_g"], f32).reshape(H),
        np.asarray(inputs["L0_attn_ln1_b"], f32).reshape(H),
        np.asarray(inputs["L0_attn_ln2_g"], f32).reshape(H),
        np.asarray(inputs["L0_attn_ln2_b"], f32).reshape(H),
        np.asarray(inputs["L0_attn_tm_first"], f32).reshape(H),
    ])
    shared = {
        "vecs": np.ascontiguousarray(vecs),
        "wmod": _tile_kxm(np.asarray(inputs["L0_wmod"], f32)),
        "wr": _tile_kxm(np.asarray(inputs["L0_attn_Wr"], f32).T),
        "wk": _tile_kxm(np.asarray(inputs["L0_attn_Wk"], f32).T),
        "wv": _tile_kxm(np.asarray(inputs["L0_attn_Wv"], f32).T),
        "wo": _tile_kxm(np.asarray(inputs["L0_attn_Wo"], f32).T),
        "wcr": _tile_kxm(np.asarray(inputs["L0_attn_Wcr"], f32).T),
        "wkey": _tile_kxm(np.asarray(inputs["L0_attn_Wkey"], f32).T),
        "wval": _tile_kxm(np.asarray(inputs["L0_attn_Wval"], f32).T),
        "wrouter": np.ascontiguousarray(
            np.asarray(inputs["L0_ffn_W_router"], f32)
            .reshape(16, 128, 16).transpose(1, 0, 2).reshape(128, 256)),
        "wshu": np.concatenate([_tile_kxm(np.asarray(inputs["L0_ffn_Wsh_up"], f32)[e])
                                for e in range(E_SH)]),
        "wshd": np.concatenate([_tile_kxm(np.asarray(inputs["L0_ffn_Wsh_down"], f32)[e])
                                for e in range(E_SH)]),
        "wrtu": np.concatenate([_tile_kxm(np.asarray(inputs["L0_ffn_Wrt_up"], f32)[e])
                                for e in range(E_RT)]),
        "wrtd": np.concatenate([_tile_kxm(np.asarray(inputs["L0_ffn_Wrt_down"], f32)[e])
                                for e in range(E_RT)]),
    }
    c = np.asarray(inputs["L0_c"], f32)
    in_maps = []
    for core in range(8):
        b, s = core // 2, core % 2
        t0 = s * NT
        xc = np.zeros((W, H), f32)
        xc[2:] = x[b, t0:t0 + NT]
        if s == 1:
            xc[0:2] = x[b, t0 - 2:t0]
        m = dict(shared)
        m["x0"] = np.ascontiguousarray(xc.T)
        m["crow"] = np.ascontiguousarray(c[b].reshape(16, 128))
        m["hmask"] = np.full((128, 1), float(s), f32)
        in_maps.append(m)
    return in_maps


def kernel(**inputs):
    in_maps = _prep(inputs)
    nc = build_nc()
    res = run_bass_kernel_spmd(nc, in_maps, core_ids=list(range(8)))
    y = np.empty((B, T, H), np.float32)
    for core in range(8):
        b, s = core // 2, core % 2
        y[b, s * NT:(s + 1) * NT, :] = res.results[core]["out"].T
    return y


# version 15
# speedup vs baseline: 1.0209x; 1.0141x over previous
import contextlib
import numpy as np
import concourse.bass as bass
import concourse.mybir as mybir
from concourse.bass_utils import run_bass_kernel_spmd

F32 = mybir.dt.float32
BF16 = mybir.dt.bfloat16
AF = mybir.ActivationFunctionType
ALU = mybir.AluOpType
AX = mybir.AxisListType

H = 2048
B = 4
T = 2048
NT = 1024
W = NT + 2
FC = 16
D4 = 8192
DC4 = 64
EPS = 1e-5
E_RT = 16
E_SH = 2
DCE = 8
TB = 512
IH = 1.0 / H
FCW = FC * W

ENGINES = ("tensor", "vector", "scalar", "sync", "gpsimd")


class _Sem:
    __slots__ = ("name", "cnt")

    def __init__(self, name):
        self.name = name
        self.cnt = 0


class Sched:

    def __init__(self):
        self.steps = {e: [] for e in ENGINES}
        self.sems = []

    def sem(self, name):
        s = _Sem(name)
        self.sems.append(s)
        return s

    def emit(self, eng, fn, waits=(), inc=None, inc_n=1):
        w = []
        for s, v in waits:
            assert v <= s.cnt, f"wait on future: {s.name} {v} > {s.cnt} ({eng})"
            if v > 0:
                w.append((s, v))
        self.steps[eng].append((tuple(w), fn, inc, inc_n))
        if inc is not None:
            inc.cnt += inc_n
            return inc.cnt
        return None


def _tile_kxm(wT):
    K, M = wT.shape
    KC, MB = K // 128, M // 128
    a = wT.reshape(KC, 128, MB, 128).transpose(2, 1, 0, 3).reshape(MB, 128, KC * 128)
    return np.ascontiguousarray(a)


def build_nc():
    nc = bass.Bass("TRN2", target_bir_lowering=False)

    dt = {}
    dt["x0"] = nc.dram_tensor("x0", [H, W], F32, kind="ExternalInput")
    dt["crow"] = nc.dram_tensor("crow", [16, 128], F32, kind="ExternalInput")
    dt["hmask"] = nc.dram_tensor("hmask", [128, 1], F32, kind="ExternalInput")
    dt["vecs"] = nc.dram_tensor("vecs", [10, H], F32, kind="ExternalInput")
    dt["wmod"] = nc.dram_tensor("wmod", [96, 128, FC * 128], F32, kind="ExternalInput")
    for nm in ("wr", "wk", "wv", "wo", "wcr"):
        dt[nm] = nc.dram_tensor(nm, [16, 128, FC * 128], F32, kind="ExternalInput")
    dt["wkey"] = nc.dram_tensor("wkey", [64, 128, FC * 128], F32, kind="ExternalInput")
    dt["wval"] = nc.dram_tensor("wval", [16, 128, DC4 * 128], F32, kind="ExternalInput")
    dt["wrouter"] = nc.dram_tensor("wrouter", [128, 256], F32, kind="ExternalInput")
    dt["wshu"] = nc.dram_tensor("wshu", [E_SH * DCE, 128, FC * 128], F32, kind="ExternalInput")
    dt["wshd"] = nc.dram_tensor("wshd", [E_SH * FC, 128, DCE * 128], F32, kind="ExternalInput")
    dt["wrtu"] = nc.dram_tensor("wrtu", [E_RT * DCE, 128, FC * 128], F32, kind="ExternalInput")
    dt["wrtd"] = nc.dram_tensor("wrtd", [E_RT * FC, 128, DCE * 128], F32, kind="ExternalInput")
    dt["out"] = nc.dram_tensor("out", [H, NT], F32, kind="ExternalOutput")
    dt["xasp"] = nc.dram_tensor("xasp", [H, NT], F32, kind="Internal")
    dt["xbsp"] = nc.dram_tensor("xbsp", [H, TB], F32, kind="Internal")
    dt["x1sp"] = nc.dram_tensor("x1sp", [H, NT], F32, kind="Internal")

    S = Sched()
    sWDs = [S.sem(f"sWD{i}") for i in range(3)]
    sWR = S.sem("sWR")
    sAD = S.sem("sAD")
    sPE = S.sem("sPE")
    sDV = S.sem("sDV")
    sAC = S.sem("sAC")
    sGP = S.sem("sGP")

    ctx = contextlib.ExitStack()
    sb = lambda name, shape, dtyp: ctx.enter_context(nc.sbuf_tensor(name, shape, dtyp))
    psf = [ctx.enter_context(nc.psum_tensor(f"p{i}", [128, 512], F32)) for i in range(7)]
    p7b = ctx.enter_context(nc.psum_tensor("p7b", [128, 1024], BF16))

    B1f = sb("B1f", [128, FCW], F32)
    B2f = sb("B2f", [128, FCW], F32)
    B3b = sb("B3b", [128, FCW], BF16)
    B1bt = nc.alloc_sbuf_tensor_at("B1B", [128, 2 * FCW], BF16,
                                   offset=nc.lookup_mloc(B1f).addr)
    B2bt = nc.alloc_sbuf_tensor_at("B2B", [128, 2 * FCW], BF16,
                                   offset=nc.lookup_mloc(B2f).addr)
    B1b = B1bt[:, :]
    B2b = B2bt[:, :]
    S0 = sb("S0", [128, W + 1], F32)
    S1 = sb("S1", [128, W], F32)
    BCB = sb("BCB", [128, 2 * W], BF16)
    S0B = sb("S0B", [128, W + 1], BF16)
    DT = sb("DT", [128, W], BF16)
    SMALL = sb("SMALL", [16, W], F32)
    STATROW = SMALL[0:1, 0:1026]
    PT = sb("PT", [128, 128], F32)
    AUX = sb("AUX", [128, 64], F32)
    PTX = AUX[:, 48:64]
    CONSTV = sb("CONSTV", [128, 224], F32)
    MODV = sb("MODV", [128, 96], F32)
    WQ = sb("WQ", [128, 3 * 2048], BF16)
    WKVT = sb("WKVT", [128, 1026], F32)
    WRT = sb("WRT", [128, 256], BF16)
    GATEB = sb("GATEB", [128, NT], BF16)
    IDF = sb("IDF", [128, 128], F32)
    IDB = sb("IDB", [128, 128], BF16)
    ONESF = sb("ONESF", [1, 128], F32)
    ONESB = sb("ONESB", [128, 1], BF16)
    ONESFB = sb("ONESFB", [128, 1], F32)
    SELW = sb("SELW", [16, 128], F32)
    R16 = sb("R16", [16, 128], F32)
    EPSC = sb("EPSC", [1, 1], F32)
    NEGC = sb("NEGC", [128, 1], F32)
    SSB = sb("SSB", [128, 16], BF16)
    SSTGT = sb("SSTG", [16, 128], BF16)
    HMSK = sb("HMSK", [128, 1], F32)

    CSLOT = {"mk": 0, "mv": 1, "mr": 2, "cmk": 3, "cmr": 4, "g1": 5, "b1": 6,
             "g2": 7, "b2": 8, "tmf": 9, "ca1": 10, "cb1": 11, "gac": 12,
             "gbc": 13}

    def cv(slot, fc):
        j = CSLOT[slot]
        return CONSTV[:, 16 * j + fc: 16 * j + fc + 1]

    def cvall(slot):
        j = CSLOT[slot]
        return CONSTV[:, 16 * j: 16 * j + 16]

    wq_idx = [0]
    wq_slot_pe = [0, 0, 0]
    bank_state = [None] * 7
    aux_banks = [5, 6]
    aux_i = [0]

    def wq_dma(src_ap):
        q = wq_idx[0]
        wq_idx[0] += 1
        slot = q % 3
        dst = WQ[:, slot * 2048: slot * 2048 + src_ap.shape[-1]]
        waits = []
        if wq_slot_pe[slot] > 0:
            waits.append((sPE, wq_slot_pe[slot]))
        S.emit("gpsimd", lambda g, d=dst, s=src_ap: g.dma_start(d, s),
               waits=waits, inc=sWDs[slot], inc_n=16)
        return dst, (sWDs[slot], sWDs[slot].cnt), slot

    sAD2 = S.sem("sAD2")

    def scal_dma(fn, waits=()):
        S.emit("scalar", fn, waits=waits, inc=sAD2, inc_n=16)
        v = sAD2.cnt
        S.emit("scalar", lambda sc: sc.nop(), waits=[(sAD2, v)])
        return v

    def sync_dma(fn, waits=()):
        S.emit("sync", fn, waits=waits, inc=sAD, inc_n=16)
        v = sAD.cnt
        S.emit("sync", lambda s: s.nop(), waits=[(sAD, v)])
        return v

    def bank_wait(bank):
        st = bank_state[bank]
        return [st] if st is not None else []

    def bank_done(bank, sem, val):
        bank_state[bank] = (sem, val)

    def aux_next():
        b = aux_banks[aux_i[0] % len(aux_banks)]
        aux_i[0] += 1
        return b

    v_hmsk = sync_dma(lambda s: s.dma_start(HMSK[:, :], dt["hmask"][:, :]))
    v_vecs = sync_dma(lambda s: s.dma_start(
        B2f[0:16, 0:1280].rearrange("p (j c) -> p j c", c=128),
        dt["vecs"][:, :].rearrange("j (p c) -> p j c", p=16)))
    v_crow = sync_dma(lambda s: s.dma_start(B2f[0:16, 1280:1408],
                                            dt["crow"][:, :]))
    v_x0 = sync_dma(lambda s: s.dma_start(
        B1f[:, :].rearrange("p (f t) -> p f t", t=W),
        dt["x0"][:, :].rearrange("(f p) t -> p f t", p=128)))
    S.emit("gpsimd", lambda g: g.dma_start(WRT[:, :], dt["wrouter"][:, :]),
           inc=sWR, inc_n=16)
    v_wrt = sWR.cnt

    S.emit("gpsimd", lambda g: g.iota(B2f[:, 1408:1536], [[1, 128]],
                                      channel_multiplier=-1,
                                      allow_small_or_imprecise_dtypes=True),
           waits=[(sAD, v_vecs)], inc=sGP)
    S.emit("gpsimd", lambda g: g.iota(R16[:, :], [[0, 128]],
                                      channel_multiplier=1,
                                      allow_small_or_imprecise_dtypes=True),
           inc=sGP)
    v_iota = sGP.cnt
    S.emit("vector", lambda v: v.tensor_scalar(IDF[:, :], B2f[:, 1408:1536],
                                               0.0, None, op0=ALU.is_equal),
           waits=[(sGP, v_iota)], inc=sDV)
    S.emit("vector", lambda v: v.tensor_copy(IDB[:, :], IDF[:, :]), inc=sDV)
    S.emit("vector", lambda v: v.memset(ONESF[:, :], 1.0), inc=sDV)
    S.emit("vector", lambda v: v.memset(ONESB[:, :], 1.0), inc=sDV)
    S.emit("vector", lambda v: v.memset(ONESFB[:, :], 1.0), inc=sDV)
    S.emit("vector", lambda v: v.memset(S0[:, 0:1], 0.0), inc=sDV)
    S.emit("vector", lambda v: v.memset(S0B[:, 0:1], 0.0), inc=sDV)
    S.emit("vector", lambda v: v.memset(EPSC[:, :], EPS), inc=sDV)
    S.emit("vector", lambda v: v.memset(NEGC[:, :], -1.0e38), inc=sDV)
    v_consts = sDV.cnt

    SSTG = SSTGT[0:16, :]
    S.emit("scalar", lambda sc: sc.activation(SSTG, B2f[0:16, 1280:1408],
                                              AF.Silu, bias=0.0, scale=1.0),
           waits=[(sAD, v_crow)], inc=sAC)
    v_sstg = sAC.cnt
    pe_v = S.emit("tensor", lambda t: t.transpose(p7b[:, 0:16], SSTG, IDB[0:16, 0:16]),
                  waits=[(sAC, v_sstg), (sDV, v_consts)], inc=sPE)
    v_ssb = S.emit("scalar", lambda sc: sc.copy(SSB[:, :], p7b[:, 0:16]),
                   waits=[(sPE, pe_v)], inc=sAC)

    VEC_ORDER = ["mk", "mv", "mr", "cmk", "cmr", "g1", "b1", "g2", "b2", "tmf"]
    for j, nm in enumerate(VEC_ORDER):
        bk = aux_next()
        wl = [(sAD, v_vecs), (sDV, v_consts)] + bank_wait(bk)
        pe_v = S.emit("tensor",
                      lambda t, j_=j, b=bk: t.transpose(
                          psf[b][0:128, 0:16], B2f[0:16, j_ * 128:(j_ + 1) * 128],
                          IDF[0:16, 0:16]),
                      waits=wl, inc=sPE)
        ac_v = S.emit("scalar",
                      lambda sc, nm_=nm, b=bk: sc.copy(cvall(nm_), psf[b][0:128, 0:16]),
                      waits=[(sPE, pe_v)], inc=sAC)
        bank_done(bk, sAC, ac_v)

    sWM = [S.sem("sWM0"), S.sem("sWM1")]
    wm_half_pe = [0, 0]
    lna_sum_pe = [0, 0, 0]
    lna_bank_first = [True, True, True]
    pend_tq = []

    def _lna_sums(fc):
        for nb in range(3):
            nbw = [342, 342, 342][nb]
            w0 = nb * 342
            bk = 4 + nb
            waits = [(sAD, v_x0)]
            if lna_bank_first[nb]:
                waits += bank_wait(bk)
            r = S.emit(
                "tensor",
                lambda t, b=bk, st=lna_bank_first[nb], sp=(fc == FC - 1),
                f=fc, w0_=w0, n_=nbw:
                t.matmul(psf[b][0:1, 0:n_], ONESFB[:, :],
                         B1f[:, f * W + w0_: f * W + w0_ + n_],
                         start=st, stop=sp),
                waits=waits, inc=(sPE if fc == FC - 1 else None))
            lna_bank_first[nb] = False
            if r is not None:
                lna_sum_pe[nb] = r

    for g in range(24):
        half = g % 2
        hbase = half * 8192
        wl_d = [(sPE, wm_half_pe[half])] if wm_half_pe[half] else []
        S.emit("gpsimd", lambda gg, h=hbase, g_=g: gg.dma_start(
            B3b[:, h:h + 8192].rearrange("p (q c) -> p q c", c=2048),
            dt["wmod"][g_ * 4:(g_ + 1) * 4, :, :].transpose([1, 0, 2])),
            waits=wl_d, inc=sWM[half], inc_n=16)
        v_rdy = sWM[half].cnt
        bk = g % 2
        wl = [(sWM[half], v_rdy), (sAC, v_ssb)] + bank_wait(bk)
        pe_v = None
        for k in range(FC):
            r = S.emit("tensor",
                       lambda t, h=hbase, k_=k, b=bk, st=(k == 0),
                       sp=(k == FC - 1):
                       t.matmul(psf[b][0:1, 0:512],
                                SSB[:, k_:k_ + 1],
                                B3b[:, h:h + 8192]
                                .rearrange("p (q c) -> p q c", c=2048)
                                [:, :, k_ * 128:(k_ + 1) * 128],
                                start=st, stop=sp),
                       waits=(wl if k == 0 else []),
                       inc=(sPE if k == FC - 1 else None))
            pe_v = r if r is not None else pe_v
        wm_half_pe[half] = pe_v
        st0 = (g % 2) * 512
        ac_row = S.emit("scalar", lambda sc, b=bk, s0=st0: sc.copy(
            S1[0:1, s0:s0 + 512], psf[b][0:1, 0:512]),
            waits=[(sPE, pe_v)], inc=sAC)
        bank_done(bk, sAC, ac_row)
        pend_tq.append((g, st0, ac_row))
        if len(pend_tq) > 1 or g == 23:
            for (gq, s0q, acq) in pend_tq[:-1] if g != 23 else pend_tq:
                for tq in range(4):
                    Mb = gq * 4 + tq
                    bt = 2 + tq % 2
                    pe_t = S.emit("tensor",
                                  lambda t, b=bt, c=s0q + tq * 128: t.transpose(
                                      psf[b][:, 0:1], S1[0:1, c:c + 128],
                                      IDF[0:1, 0:1]),
                                  waits=[(sAC, acq)] + bank_wait(bt), inc=sPE)
                    ac_v = S.emit("scalar", lambda sc, b=bt, m=Mb: sc.copy(
                        MODV[:, m:m + 1], psf[b][:, 0:1]),
                        waits=[(sPE, pe_t)], inc=sAC)
                    bank_done(bt, sAC, ac_v)
            pend_tq[:] = (pend_tq[-1:] if g != 23 else [])
        if 2 <= g <= 17:
            _lna_sums(g - 2)
    for bk in (4, 5, 6):
        bank_done(bk, sPE, lna_sum_pe[bk - 4])
    v_mod = sAC.cnt

    S.emit("vector", lambda v: v.tensor_scalar(cvall("ca1"), MODV[:, 16:32],
                                               1.0, None, op0=ALU.add),
           waits=[(sAC, v_mod)], inc=sDV)
    S.emit("vector", lambda v: v.tensor_scalar(cvall("cb1"), MODV[:, 64:80],
                                               1.0, None, op0=ALU.add), inc=sDV)
    v_modpost = sDV.cnt
    for src0, dst in ((32, "gac"), (80, "gbc")):
        g = MODV[:, src0:src0 + 16]
        S.emit("vector", lambda v, g_=g: v.tensor_scalar(
            PTX, g_, 0.25, None, op0=ALU.max), inc=sDV)
        S.emit("vector", lambda v, g_=g: v.tensor_scalar(
            AUX[:, 0:16], g_, -0.25, None, op0=ALU.min), inc=sDV)
        S.emit("vector", lambda v, g_=g: v.tensor_scalar(
            AUX[:, 16:32], g_, 0.0, None, op0=ALU.is_ge), inc=sDV)
        S.emit("vector", lambda v: v.tensor_tensor(
            PTX, PTX, AUX[:, 0:16], op=ALU.subtract), inc=sDV)
        S.emit("vector", lambda v: v.tensor_tensor(
            PTX, PTX, AUX[:, 16:32], op=ALU.mult), inc=sDV)
        v_modpost = S.emit("vector", lambda v, d=dst: v.tensor_tensor(
            cvall(d), AUX[:, 0:16], PTX, op=ALU.add), inc=sDV)

    stats_sq_guard = [0]

    def emit_stats_phase(src_chunk_fn, scr_big_bf, width, nblocks, src_waits,
                         src_waits_fn=None, slot_guard=True, presums=None):
        del scr_big_bf
        del nblocks
        nbws = [342, 342, width - 684]
        nblocks = 3

        def w0_of(nb):
            return sum(nbws[:nb])

        if presums is not None:
            sum_pe = list(presums)
        else:
            sum_pe = [0] * nblocks
        for nb in (() if presums is not None else range(nblocks)):
            nbw = nbws[nb]
            w0 = w0_of(nb)
            bk = 4 + nb
            wl0 = bank_wait(bk)
            for fc in range(FC):
                wls = src_waits_fn(fc) if src_waits_fn else src_waits
                waits = (wl0 + wls) if fc == 0 else wls
                r = S.emit(
                    "tensor",
                    lambda t, a=src_chunk_fn(fc), b=bk, st=(fc == 0),
                    sp=(fc == FC - 1), w0_=w0, n_=nbw:
                    t.matmul(psf[b][0:1, 0:n_], ONESFB[:, :],
                             a[:, w0_:w0_ + n_], start=st, stop=sp),
                    waits=waits, inc=(sPE if fc == FC - 1 else None))
                sum_pe[nb] = r if r is not None else sum_pe[nb]
        slot_mark = [stats_sq_guard[0], stats_sq_guard[0]]
        sq_pe = [0] * nblocks
        for fc in range(FC):
            slot = WKVT[:, 0:width] if fc % 2 == 0 else S0[:, 1:1 + width]
            wls = src_waits_fn(fc) if src_waits_fn else src_waits
            wl = list(wls) + [(sPE, slot_mark[fc % 2])]
            if fc < 2 and slot_guard:
                wl += list(src_waits)
            a1 = S.emit("scalar", lambda sc, s=src_chunk_fn(fc), d=slot:
                        sc.activation(d, s, AF.Square, bias=0.0, scale=1.0),
                        waits=wl, inc=sAC)
            last = 0
            for nb in range(nblocks):
                nbw = nbws[nb]
                w0 = w0_of(nb)
                bk = nb
                waits = [(sAC, a1)] + (bank_wait(bk) if fc == 0 else [])
                r = S.emit(
                    "tensor",
                    lambda t, d=slot, b=bk, st=(fc == 0), sp=(fc == FC - 1),
                    w0_=w0, n_=nbw:
                    t.matmul(psf[b][0:1, 0:n_], ONESFB[:, :],
                             d[:, w0_:w0_ + n_], start=st, stop=sp),
                    waits=waits,
                    inc=(sPE if nb == nblocks - 1 or fc == FC - 1 else None))
                if r is not None:
                    last = r
                    sq_pe[nb] = max(sq_pe[nb], r)
            slot_mark[fc % 2] = last
        stats_sq_guard[0] = max(slot_mark)
        v_bc = 0
        prev_bc_pe = 0
        for nb in range(nblocks):
            nbw = nbws[nb]
            w0 = w0_of(nb)
            Cm = STATROW[:, 0:nbw]
            Bv = STATROW[:, 342:342 + nbw]
            Dr = STATROW[:, 684:684 + nbw]
            wl1 = [(sPE, sum_pe[nb])]
            if prev_bc_pe:
                wl1.append((sPE, prev_bc_pe))
            dv1 = S.emit("vector", lambda v, b=4 + nb, c_=Cm, n_=nbw:
                v.tensor_scalar(c_, psf[b][0:1, 0:n_], IH, None, op0=ALU.mult),
                waits=wl1, inc=sDV)
            dv2 = S.emit("vector", lambda v, b=nb, bv=Bv, n_=nbw:
                v.tensor_scalar(bv, psf[b][0:1, 0:n_], IH, None, op0=ALU.mult),
                waits=[(sPE, sq_pe[nb])], inc=sDV)
            S.emit("vector", lambda v, d=Dr, c_=Cm: v.tensor_tensor(
                d, c_, c_, op=ALU.mult), inc=sDV)
            dv4 = S.emit("vector", lambda v, bv=Bv, d=Dr: v.tensor_tensor(
                bv, bv, d, op=ALU.subtract), inc=sDV)
            S.emit("scalar", lambda sc, d=Dr, bv=Bv: sc.activation(
                d, bv, AF.Ln, bias=EPSC[:, :], scale=1.0),
                waits=[(sDV, dv4)], inc=sAC)
            ac2 = S.emit("scalar", lambda sc, d=Dr: sc.activation(
                d, d, AF.Exp, bias=0.0, scale=-0.5), inc=sAC)
            dv5 = S.emit("vector", lambda v, c_=Cm, d=Dr: v.tensor_tensor(
                c_, c_, d, op=ALU.mult), waits=[(sAC, ac2)], inc=sDV)
            bank_done(4 + nb, sDV, dv1)
            bank_done(nb, sDV, dv2)
            for which, srcslot in ((0, Dr), (1, Cm)):
                bkb = 3
                wl = bank_wait(bkb) + [(sDV, dv5)]
                pe_v = S.emit("tensor", lambda t, b=bkb, s=srcslot, n_=nbw:
                              t.matmul(psf[b][:, 0:n_], ONESF[0:1, :], s,
                                       start=True, stop=True), waits=wl, inc=sPE)
                prev_bc_pe = pe_v
                v_bc = S.emit("scalar", lambda sc, b=bkb, wh=which, w0_=w0, n_=nbw:
                              sc.copy(BCB[:, wh * W + w0_: wh * W + w0_ + n_],
                                      psf[b][:, 0:n_]),
                              waits=[(sPE, pe_v)], inc=sAC)
                bank_done(bkb, sAC, v_bc)
        return v_bc

    v_lna = emit_stats_phase(lambda fc: B1f[:, fc * W: fc * W + W],
                             B2b, W, 3, [(sAD, v_x0)], presums=lna_sum_pe)

    v_h = 0
    h_marks = []
    for fc in range(FC):
        src = B1f[:, fc * W: fc * W + W]
        dst = B2f[:, fc * W: fc * W + W]
        S.emit("vector", lambda v, s=src: v.tensor_tensor(
            DT[:, 0:W], s, BCB[:, 0:W], op=ALU.mult),
            waits=[(sAC, v_lna), (sDV, v_modpost)], inc=sDV)
        S.emit("vector", lambda v: v.tensor_tensor(
            DT[:, 0:W], DT[:, 0:W], BCB[:, W:2 * W], op=ALU.subtract), inc=sDV)
        v_h = S.emit("vector", lambda v, d=dst, f=fc: v.tensor_scalar(
            d, DT[:, 0:W], cv("ca1", f), MODV[:, f:f + 1],
            op0=ALU.mult, op1=ALU.add), inc=sDV)
        h_marks.append(v_h)

    v_ln1 = emit_stats_phase(lambda fc: B2f[:, fc * W: fc * W + W],
                             B1b, W, 3, [(sDV, v_h)],
                             src_waits_fn=lambda fc: [(sDV, h_marks[fc])],
                             slot_guard=False)

    def mixA(fc, width=W):
        return B1b[:, fc * W: fc * W + width]

    def mixB(fc, width=W):
        return B1b[:, FCW + fc * W: FCW + fc * W + width]

    def emit_mix_chunk(fc, v_bcast, src_big, gslot, bslot, mixes, width=W,
                       extra_waits=()):
        last = 0
        if True:
            src = src_big[:, fc * W: fc * W + W]
            wl0 = [(sAC, v_bcast), (sAD, v_hmsk)] + list(extra_waits)
            S.emit("vector", lambda v, s=src: v.tensor_tensor(
                S0B[:, 1:W + 1], s, BCB[:, 0:W], op=ALU.mult),
                waits=wl0, inc=sDV)
            S.emit("vector", lambda v: v.tensor_tensor(
                S0B[:, 1:W + 1], S0B[:, 1:W + 1], BCB[:, W:2 * W],
                op=ALU.subtract), inc=sDV)
            S.emit("vector", lambda v, f=fc, g=gslot, b_=bslot: v.tensor_scalar(
                S0B[:, 1:W + 1], S0B[:, 1:W + 1], cv(g, f), cv(b_, f),
                op0=ALU.mult, op1=ALU.add), inc=sDV)
            S.emit("vector", lambda v: v.tensor_scalar(
                S0B[:, 2:3], S0B[:, 2:3], HMSK[:, 0:1], None, op0=ALU.mult),
                inc=sDV)
            S.emit("vector", lambda v: v.tensor_tensor(
                DT[:, 0:W], S0B[:, 1:W + 1], S0B[:, 0:W], op=ALU.subtract),
                inc=sDV)
            for slotname, dest_fn in mixes:
                if width == W:
                    last = S.emit("vector", lambda v, f=fc, sn=slotname, df=dest_fn:
                                  v.scalar_tensor_tensor(
                                      df(f), DT[:, 0:W], cv(sn, f), S0B[:, 0:W],
                                      op0=ALU.mult, op1=ALU.add), inc=sDV)
                else:
                    last = S.emit("vector", lambda v, f=fc, sn=slotname, df=dest_fn:
                                  v.scalar_tensor_tensor(
                                      df(f), DT[:, 2:2 + NT], cv(sn, f),
                                      S0B[:, 2:2 + NT], op0=ALU.mult, op1=ALU.add),
                                  inc=sDV)
        return last

    def emit_mix_pair(v_bcast, src_big, gslot, bslot, mixes, width=W,
                      extra_waits=()):
        last = 0
        for fc in range(FC):
            last = emit_mix_chunk(fc, v_bcast, src_big, gslot, bslot, mixes,
                                  width, extra_waits if fc == 0 else ())
        return last

    v_mix_rk = emit_mix_pair(v_ln1, B2f, "g1", "b1",
                             [("mr", lambda fc: mixA(fc)),
                              ("mk", lambda fc: mixB(fc))])

    def emit_hxh(wname, moving_fn, evac_fn, v_moving, nblk=3, width=W,
                 extra_mm_waits=(), after_mc=None):
        nbw = width // nblk
        for mc in range(FC):
            slot_ap, rdy, slot = wq_dma(dt[wname][mc, :, :])
            pe_last = None
            for nb in range(nblk):
                c0 = nb * nbw
                bank = (mc * nblk + nb) % 4
                wl = [rdy, (sDV, v_moving)] + bank_wait(bank) + \
                    list(extra_mm_waits)
                for k in range(FC):
                    r = S.emit(
                        "tensor",
                        lambda t, sl=slot_ap, k_=k, b=bank, c=c0,
                        st=(k == 0), sp=(k == FC - 1), mf=moving_fn:
                        t.matmul(psf[b][:, 0:nbw], sl[:, k_ * 128:(k_ + 1) * 128],
                                 mf(k_, c, nbw), start=st, stop=sp),
                        waits=(wl if k == 0 else []),
                        inc=(sPE if k == FC - 1 else None))
                    pe_last = r if r is not None else pe_last
                sem, val = evac_fn(mc, nb, c0, nbw, psf[bank], [(sPE, pe_last)])
                bank_done(bank, sem, val)
            wq_slot_pe[slot] = pe_last
            if after_mc is not None:
                after_mc(mc)

    def evac_sigr(mc, nb, c0, cw, ps, waits):
        v = S.emit("scalar", lambda sc, m=mc, c=c0, w=cw, p=ps:
                   sc.activation(B3b[:, m * W + c: m * W + c + w], p[:, 0:w],
                                 AF.Sigmoid, bias=0.0, scale=1.0),
                   waits=waits, inc=sAC)
        return sAC, v

    emit_hxh("wr", lambda k, c0, nw: mixA(k)[:, c0:c0 + nw], evac_sigr, v_mix_rk)
    v_sigr = sAC.cnt
    v_r_pe = sPE.cnt

    wkv_prev_d4 = [0]

    def evac_wkv(mc, nb, c0, cw, ps, waits):
        wl_prev = [(sDV, wkv_prev_d4[0])] if wkv_prev_d4[0] else []
        a1 = S.emit("scalar", lambda sc, p=ps, m=mc, w=cw:
                    sc.activation(WKVT[:, 0:w], p[:, 0:w], AF.Identity,
                                  bias=cv("tmf", m), scale=1.0),
                    waits=waits + wl_prev, inc=sAC)
        S.emit("scalar", lambda sc, w=cw: sc.activation(
            WKVT[:, 342:342 + w], WKVT[:, 0:w], AF.Relu, bias=NEGC[:, :],
            scale=-1.0), inc=sAC)
        S.emit("scalar", lambda sc, w=cw: sc.activation(
            GATEB[:, 0:w], WKVT[:, 342:342 + w], AF.Exp, bias=0.0, scale=-1.0),
            inc=sAC)
        a3 = S.emit("scalar", lambda sc, w=cw: sc.activation(
            GATEB[:, 342:342 + w], WKVT[:, 342:342 + w], AF.Exp, bias=0.0,
            scale=1.0), inc=sAC)
        S.emit("vector", lambda v, w=cw: v.tensor_tensor(
            GATEB[:, 0:w], GATEB[:, 0:w], GATEB[:, 342:342 + w], op=ALU.mult),
            waits=[(sAC, a3)], inc=sDV)
        d4 = S.emit("vector", lambda v, m=mc, c=c0, w=cw: v.tensor_tensor(
            B3b[:, m * W + c: m * W + c + w], B3b[:, m * W + c: m * W + c + w],
            GATEB[:, 0:w], op=ALU.mult), inc=sDV)
        wkv_prev_d4[0] = d4
        return sDV, d4

    mixv_last = [0]

    def _mixv_chunk(mc):
        mixv_last[0] = emit_mix_chunk(
            mc, v_ln1, B2f, "g1", "b1", [("mv", lambda fc: mixA(fc))],
            extra_waits=([(sPE, v_r_pe)] if mc == 0 else ()))

    emit_hxh("wk", lambda k, c0, nw: mixB(k)[:, c0:c0 + nw], evac_wkv, v_mix_rk,
             extra_mm_waits=[(sAC, v_sigr)], after_mc=_mixv_chunk)
    v_gate = sDV.cnt
    v_mix_v = mixv_last[0]

    def evac_gin(mc, nb, c0, cw, ps, waits):
        v = S.emit("vector", lambda vv, m=mc, c=c0, w=cw, p=ps:
                   vv.tensor_tensor(mixB(m)[:, c: c + w], p[:, 0:w],
                                    B3b[:, m * W + c: m * W + c + w], op=ALU.mult),
                   waits=waits + [(sDV, v_gate)], inc=sDV)
        return sDV, v

    emit_hxh("wv", lambda k, c0, nw: mixA(k)[:, c0:c0 + nw], evac_gin, v_mix_v)
    v_gin = sDV.cnt

    xa_marks = [0] * FC

    def evac_xa(mc, nb, c0, cw, ps, waits):
        v = S.emit("vector", lambda vv, m=mc, c=c0, w=cw, p=ps:
                   vv.tensor_tensor(B2f[:, m * W + c: m * W + c + w],
                                    B2f[:, m * W + c: m * W + c + w],
                                    p[:, 0:w], op=ALU.add), waits=waits, inc=sDV)
        xa_marks[mc] = v
        return sDV, v

    emit_hxh("wo", lambda k, c0, nw: mixB(k)[:, c0:c0 + nw], evac_xa, v_gin)
    v_xa = sDV.cnt

    v_ln2 = emit_stats_phase(lambda fc: B2f[:, fc * W: fc * W + W],
                             B1b, W, 3, [(sDV, v_xa)],
                             src_waits_fn=lambda fc: [(sDV, xa_marks[fc])],
                             slot_guard=False)

    v_mix_ck = emit_mix_pair(v_ln2, B2f, "g2", "b2",
                             [("cmr", lambda fc: mixA(fc, NT)),
                              ("cmk", lambda fc: mixB(fc, NT))], width=NT)

    def evac_scr(mc, nb, c0, cw, ps, waits):
        v = S.emit("scalar", lambda sc, m=mc, c=c0, w=cw, p=ps:
                   sc.activation(B3b[:, m * NT + c: m * NT + c + w], p[:, 0:w],
                                 AF.Sigmoid, bias=0.0, scale=1.0),
                   waits=waits, inc=sAC)
        return sAC, v

    emit_hxh("wcr", lambda k, c0, nw: mixA(k, NT)[:, c0:c0 + nw], evac_scr,
             v_mix_ck, nblk=2, width=NT, extra_mm_waits=[(sDV, v_gin)])
    v_scr = sAC.cnt

    v_xasp = sync_dma(lambda s: s.dma_start(
        dt["xasp"][:, :].rearrange("(f p) t -> p f t", p=128),
        B2f[:, :].rearrange("p (f t) -> p f t", t=W)[:, :, 2:W]),
        waits=[(sDV, v_mix_ck)])

    v_xb = [0, 0]
    v_xbsp = 0
    xb1_marks = []
    for tb in range(2):
        c0tb = tb * TB
        wl = [(sAD, v_xasp), (sAC, v_scr)]
        if tb == 1:
            wl.append((sAD, v_xbsp))
        v_xare = sync_dma(lambda s, c=c0tb: s.dma_start(
            B1f[:, 0:FC * TB].rearrange("p (f t) -> p f t", t=TB),
            dt["xasp"][:, :].rearrange("(f p) t -> p f t", p=128)[:, :, c:c + TB]),
            waits=wl)

        v_skk = 0
        for mg in range(DC4):
            slot_ap, rdy, slot = wq_dma(dt["wkey"][mg, :, :])
            bank = mg % 4
            wl2 = [rdy, (sDV, v_mix_ck), (sAD, v_xasp)] + bank_wait(bank)
            pe_last = None
            for k in range(FC):
                r = S.emit(
                    "tensor",
                    lambda t, sl=slot_ap, k_=k, b=bank, c=c0tb,
                    st=(k == 0), sp=(k == FC - 1):
                    t.matmul(psf[b][:, 0:TB], sl[:, k_ * 128:(k_ + 1) * 128],
                             mixB(k_, NT)[:, c:c + TB], start=st, stop=sp),
                    waits=(wl2 if k == 0 else []),
                    inc=(sPE if k == FC - 1 else None))
                pe_last = r if r is not None else pe_last
            wq_slot_pe[slot] = pe_last
            v_skk = S.emit("scalar", lambda sc, b=bank, m=mg:
                           sc.activation(B2b[:, m * TB:(m + 1) * TB],
                                         psf[b][:, 0:TB], AF.Silu,
                                         bias=0.0, scale=1.0),
                           waits=[(sPE, pe_last)], inc=sAC)
            bank_done(bank, sAC, v_skk)

        for m in range(FC):
            bank = m % 4
            first = True
            pe_last = None
            for kq in range(4):
                slot_ap, rdy, slot = wq_dma(dt["wval"][m, :, kq * 2048:(kq + 1) * 2048])
                wl3 = [rdy, (sAC, v_skk)] + (bank_wait(bank) if first else [])
                for kk in range(16):
                    g = kq * 16 + kk
                    r = S.emit(
                        "tensor",
                        lambda t, sl=slot_ap, k_=kk, g_=g, b=bank,
                        st=first, sp=(kq == 3 and kk == 15):
                        t.matmul(psf[b][:, 0:TB], sl[:, k_ * 128:(k_ + 1) * 128],
                                 B2b[:, g_ * TB:(g_ + 1) * TB], start=st, stop=sp),
                        waits=(wl3 if kk == 0 else []),
                        inc=(sPE if kk == 15 else None))
                    pe_last = r if r is not None else pe_last
                    first = False
                wq_slot_pe[slot] = pe_last
            S.emit("vector", lambda v, b=bank, m_=m, c=c0tb:
                   v.tensor_tensor(S1[:, 0:TB], psf[b][:, 0:TB],
                                   B3b[:, m_ * NT + c: m_ * NT + c + TB],
                                   op=ALU.mult),
                   waits=[(sPE, pe_last), (sAC, v_scr), (sAD, v_xare)], inc=sDV)
            dv = S.emit("vector", lambda v, m_=m:
                        v.tensor_tensor(B1f[:, m_ * TB:(m_ + 1) * TB], S1[:, 0:TB],
                                        B1f[:, m_ * TB:(m_ + 1) * TB], op=ALU.add),
                        inc=sDV)
            bank_done(bank, sDV, dv)
            if tb == 1:
                xb1_marks.append(dv)
        v_xb[tb] = sDV.cnt

        if tb == 0:
            v_xbsp = sync_dma(lambda s: s.dma_start(
                dt["xbsp"][:, :].rearrange("(f p) t -> p f t", p=128),
                B1f[:, 0:FC * TB].rearrange("p (f t) -> p f t", t=TB)),
                waits=[(sDV, v_xb[0])])

    v_xb1cp = 0
    for fc in range(FC):
        v_xb1cp = S.emit("vector", lambda v, f=fc: v.tensor_copy(
            B2f[:, f * W + 2 + TB: f * W + 2 + 2 * TB], B1f[:, f * TB:(f + 1) * TB]),
            waits=[(sDV, xb1_marks[fc])], inc=sDV)
    v_xbre = scal_dma(lambda s: s.dma_start(
        B2f[:, :].rearrange("p (f t) -> p f t", t=W)[:, :, 2:2 + TB],
        dt["xbsp"][:, :].rearrange("(f p) t -> p f t", p=128)),
        waits=[(sDV, v_xb1cp)])
    v_x0re = sync_dma(lambda s: s.dma_start(
        B1f[:, 0:FC * NT].rearrange("p (f t) -> p f t", t=NT),
        dt["x0"][:, :].rearrange("(f p) t -> p f t", p=128)[:, :, 2:W]),
        waits=[(sDV, v_xb1cp)])

    v_x1 = 0
    x1_marks = []
    for fc in range(FC):
        v_x1 = S.emit("vector", lambda v, f=fc: v.scalar_tensor_tensor(
            B2f[:, f * W + 2: f * W + 2 + NT],
            B2f[:, f * W + 2: f * W + 2 + NT], cv("gac", f),
            B1f[:, f * NT:(f + 1) * NT], op0=ALU.mult, op1=ALU.add),
            waits=[(sAD2, v_xbre), (sAD, v_x0re)], inc=sDV)
        x1_marks.append(v_x1)

    v_lna2 = emit_stats_phase(lambda fc: B2f[:, fc * W + 2: fc * W + 2 + NT],
                              B1b, NT, 2, [(sDV, v_x1)],
                              src_waits_fn=lambda fc: [(sDV, x1_marks[fc])],
                              slot_guard=False)

    v_hf = 0
    for fc in range(FC):
        src = B2f[:, fc * W + 2: fc * W + 2 + NT]
        S.emit("vector", lambda v, s=src: v.tensor_tensor(
            DT[:, 0:NT], s, BCB[:, 0:NT], op=ALU.mult),
            waits=[(sAC, v_lna2)], inc=sDV)
        S.emit("vector", lambda v: v.tensor_tensor(
            DT[:, 0:NT], DT[:, 0:NT], BCB[:, W:W + NT], op=ALU.subtract), inc=sDV)
        v_hf = S.emit("vector", lambda v, f=fc: v.tensor_scalar(
            B3b[:, f * NT:(f + 1) * NT], DT[:, 0:NT], cv("cb1", f),
            MODV[:, 48 + f:49 + f], op0=ALU.mult, op1=ALU.add), inc=sDV)

    v_logit = 0
    for nb in range(2):
        bk = aux_next()
        wl = [(sWR, v_wrt), (sDV, v_hf)] + bank_wait(bk)
        pe_last = None
        for k in range(FC):
            r = S.emit("tensor",
                       lambda t, k_=k, b=bk, c=nb * 512,
                       st=(k == 0), sp=(k == FC - 1):
                       t.matmul(psf[b][0:16, 0:512],
                                WRT[:, k_ * 16:(k_ + 1) * 16],
                                B3b[:, k_ * NT + c: k_ * NT + c + 512],
                                start=st, stop=sp),
                       waits=(wl if k == 0 else []),
                       inc=(sPE if k == FC - 1 else None))
            pe_last = r if r is not None else pe_last
        v_logit = S.emit("scalar", lambda sc, b=bk, c=nb * 512:
                         sc.copy(SMALL[0:16, c:c + 512], psf[b][0:16, 0:512]),
                         waits=[(sPE, pe_last)], inc=sAC)
        bank_done(bk, sAC, v_logit)

    v_gatefm = 0
    for t8 in range(8):
        bk = aux_next()
        wl = [(sAC, v_logit)] + bank_wait(bk)
        pe_v = S.emit("tensor", lambda t, b=bk, i=t8: t.transpose(
            psf[b][0:128, 0:16], SMALL[0:16, i * 128:(i + 1) * 128],
            IDF[0:16, 0:16]), waits=wl, inc=sPE)
        ac_v = S.emit("scalar", lambda sc, b=bk, i=t8: sc.copy(
            PT[:, i * 16:(i + 1) * 16], psf[b][0:128, 0:16]),
            waits=[(sPE, pe_v)], inc=sAC)
        bank_done(bk, sAC, ac_v)
        tile = PT[:, t8 * 16:(t8 + 1) * 16]
        gtile = tile
        c = t8
        S.emit("vector", lambda v, tl=tile, c_=c: v.reduce_max(
            AUX[:, c_:c_ + 1], tl, axis=AX.X), waits=[(sAC, ac_v)], inc=sDV)
        nmx = S.emit("vector", lambda v, c_=c: v.tensor_scalar(
            AUX[:, 8 + c_:9 + c_], AUX[:, c_:c_ + 1], -1.0, None, op0=ALU.mult),
            inc=sDV)
        a1 = S.emit("scalar", lambda sc, tl=tile, c_=c: sc.activation(
            tl, tl, AF.Exp, bias=AUX[:, 8 + c_:9 + c_], scale=1.0),
            waits=[(sDV, nmx)], inc=sAC)
        S.emit("vector", lambda v, tl=tile, c_=c: v.reduce_sum(
            AUX[:, 16 + c_:17 + c_], tl, axis=AX.X), waits=[(sAC, a1)], inc=sDV)
        S.emit("vector", lambda v, c_=c: v.reciprocal(
            AUX[:, 24 + c_:25 + c_], AUX[:, 16 + c_:17 + c_]), inc=sDV)
        S.emit("vector", lambda v, tl=tile, c_=c: v.tensor_scalar(
            tl, tl, AUX[:, 24 + c_:25 + c_], None, op0=ALU.mult), inc=sDV)
        S.emit("vector", lambda v, tl=tile, c_=c: v.reduce_max(
            AUX[:, 32 + c_:33 + c_], tl, axis=AX.X), inc=sDV)
        S.emit("vector", lambda v, tl=tile, c_=c: v.tensor_scalar(
            PTX, tl, AUX[:, 32 + c_:33 + c_], None, op0=ALU.is_lt),
            inc=sDV)
        S.emit("vector", lambda v, tl=tile: v.tensor_tensor(
            PTX, tl, PTX, op=ALU.mult), inc=sDV)
        S.emit("vector", lambda v, c_=c: v.reduce_max(
            AUX[:, 40 + c_:41 + c_], PTX, axis=AX.X), inc=sDV)
        S.emit("vector", lambda v, tl=tile, c_=c: v.tensor_scalar(
            PTX, tl, AUX[:, 40 + c_:41 + c_], None, op0=ALU.is_ge),
            inc=sDV)
        dvg = S.emit("vector", lambda v, tl=tile, g=gtile: v.tensor_tensor(
            g, tl, PTX, op=ALU.mult), inc=sDV)
        bk2 = aux_next()
        wl2 = [(sDV, dvg)] + bank_wait(bk2)
        pe2 = S.emit("tensor", lambda t, b=bk2, g=gtile: t.transpose(
            psf[b][0:16, 0:128], g, IDF[:, :]), waits=wl2, inc=sPE)
        v_gatefm = S.emit("scalar", lambda sc, b=bk2, i=t8: sc.copy(
            SMALL[0:16, i * 128:(i + 1) * 128], psf[b][0:16, 0:128]),
            waits=[(sPE, pe2)], inc=sAC)
        bank_done(bk2, sAC, v_gatefm)

    v_x1sp = sync_dma(lambda s: s.dma_start(
        dt["x1sp"][:, :].rearrange("(f p) t -> p f t", p=128),
        B2f[:, :].rearrange("p (f t) -> p f t", t=W)[:, :, 2:W]),
        waits=[(sDV, v_x1)])

    GROUPS = [[("sh", 0), ("sh", 1), ("rt", 0), ("rt", 1)],
              [("rt", 2), ("rt", 3), ("rt", 4), ("rt", 5)],
              [("rt", 6), ("rt", 7), ("rt", 8), ("rt", 9)],
              [("rt", 10), ("rt", 11), ("rt", 12), ("rt", 13)],
              [("rt", 14), ("rt", 15)]]

    def mid_ap(q, dc, c0, cw):
        base = q * (DCE * NT) + dc * NT + c0
        return B1b[:, base: base + cw]

    aux_banks[:] = [6]
    racc_started = False
    gateb_state = [None]
    selw_pe = [0]
    first_up = [True]

    for gi, grp in enumerate(GROUPS):
        gb_ready = {}
        grp_mid_sem = None
        grp_mid_val = 0
        for qi, (kind, e) in enumerate(grp):
            if kind == "rt":
                slot_g = 0
                wl0 = [(sGP, v_iota)]
                if selw_pe[0]:
                    wl0.append((sPE, selw_pe[0]))
                dv_sel = S.emit("vector", lambda v, e_=e: v.tensor_scalar(
                    SELW[:, :], R16[:, :], float(e_), None, op0=ALU.is_equal),
                    waits=wl0, inc=sDV)
                for nb in range(2):
                    bk = aux_next()
                    wl = [(sAC, v_gatefm), (sDV, dv_sel)] + bank_wait(bk)
                    if gateb_state[slot_g] is not None:
                        wl.append(gateb_state[slot_g])
                    pe_v = S.emit("tensor", lambda t, b=bk, c=nb * 512:
                                  t.matmul(psf[b][:, 0:512], SELW[:, :],
                                           SMALL[0:16, c:c + 512],
                                           start=True, stop=True), waits=wl, inc=sPE)
                    selw_pe[0] = pe_v
                    ac_v = S.emit("scalar", lambda sc, b=bk, c=nb * 512:
                                  sc.copy(GATEB[:, c: c + 512],
                                          psf[b][:, 0:512]),
                                  waits=[(sPE, pe_v)], inc=sAC)
                    bank_done(bk, sAC, ac_v)
                gb_ready[e] = sAC.cnt
            wnm = "wshu" if kind == "sh" else "wrtu"
            exp_last = None
            for dc in range(DCE):
                slot_ap, rdy, slot = wq_dma(dt[wnm][e * DCE + dc, :, :])
                for nb in range(2):
                    c0 = nb * 512
                    bank = (qi * DCE * 2 + dc * 2 + nb) % 4
                    wl = [rdy, (sDV, v_hf)] + bank_wait(bank)
                    if first_up[0]:
                        wl += [(sDV, v_x1), (sAC, v_lna2)]
                        first_up[0] = False
                    pe_last = None
                    for k in range(FC):
                        r = S.emit(
                            "tensor",
                            lambda t, sl=slot_ap, k_=k, b=bank, c=c0,
                            st=(k == 0), sp=(k == FC - 1):
                            t.matmul(psf[b][:, 0:512],
                                     sl[:, k_ * 128:(k_ + 1) * 128],
                                     B3b[:, k_ * NT + c: k_ * NT + c + 512],
                                     start=st, stop=sp),
                            waits=(wl if k == 0 else []),
                            inc=(sPE if k == FC - 1 else None))
                        pe_last = r if r is not None else pe_last
                    ac_v = S.emit("scalar", lambda sc, b=bank, q=qi, d=dc, c=c0:
                                  sc.activation(mid_ap(q, d, c, 512),
                                                psf[b][:, 0:512], AF.Gelu,
                                                bias=0.0, scale=1.0),
                                  waits=[(sPE, pe_last)], inc=sAC)
                    if kind == "rt":
                        dv_v = S.emit("vector",
                                      lambda v, q=qi, d=dc, c=c0:
                                      v.tensor_tensor(
                                          mid_ap(q, d, c, 512),
                                          mid_ap(q, d, c, 512),
                                          GATEB[:, c: c + 512],
                                          op=ALU.mult),
                                      waits=[(sAC, ac_v), (sAC, gb_ready[e])],
                                      inc=sDV)
                        bank_done(bank, sDV, dv_v)
                        grp_mid_sem, grp_mid_val = sDV, dv_v
                        exp_last = (sDV, dv_v)
                    else:
                        bank_done(bank, sAC, ac_v)
                        grp_mid_sem, grp_mid_val = sAC, ac_v
                wq_slot_pe[slot] = sPE.cnt
            if kind == "rt":
                gateb_state[0] = exp_last
        mid_wait = (grp_mid_sem, grp_mid_val)

        for mc in range(FC):
            first = [True, True]
            pe_last2 = [None, None]
            for qi, (kind, e) in enumerate(grp):
                wnm = "wshd" if kind == "sh" else "wrtd"
                slot_ap, rdy, slot = wq_dma(dt[wnm][e * FC + mc, :, :])
                for nb in range(2):
                    bank = 4 + nb
                    c0 = nb * 512
                    wl = [rdy, mid_wait]
                    if first[nb]:
                        wl += bank_wait(bank)
                    for k in range(DCE):
                        r = S.emit(
                            "tensor",
                            lambda t, sl=slot_ap, k_=k, b=bank, q=qi, c=c0,
                            st=first[nb],
                            sp=(qi == len(grp) - 1 and k == DCE - 1):
                            t.matmul(psf[b][:, 0:512],
                                     sl[:, k_ * 128:(k_ + 1) * 128],
                                     mid_ap(q, k_, c, 512), start=st, stop=sp),
                            waits=(wl if k == 0 else []),
                            inc=(sPE if k == DCE - 1 else None))
                        pe_last2[nb] = r if r is not None else pe_last2[nb]
                        first[nb] = False
                wq_slot_pe[slot] = sPE.cnt
            for nb in range(2):
                bank = 4 + nb
                c0 = nb * 512
                dst = B2f[:, mc * NT + c0: mc * NT + c0 + 512]
                if not racc_started:
                    wl = [(sPE, pe_last2[nb]), (sAD, v_x1sp), (sDV, v_hf),
                          (sAC, v_lna2)]
                    dv = S.emit("vector", lambda v, b=bank, d=dst:
                                v.tensor_copy(d, psf[b][:, 0:512]),
                                waits=wl, inc=sDV)
                else:
                    dv = S.emit("vector", lambda v, b=bank, d=dst:
                                v.tensor_tensor(d, d, psf[b][:, 0:512],
                                                op=ALU.add),
                                waits=[(sPE, pe_last2[nb])], inc=sDV)
                bank_done(bank, sDV, dv)
        racc_started = True
    v_racc = sDV.cnt

    v_x1re = sync_dma(lambda s: s.dma_start(
        B1f[:, 0:FC * NT].rearrange("p (f t) -> p f t", t=NT),
        dt["x1sp"][:, :].rearrange("(f p) t -> p f t", p=128)),
        waits=[(sDV, v_racc)])

    v_fin = 0
    for fc in range(FC):
        v_fin = S.emit("vector", lambda v, f=fc: v.scalar_tensor_tensor(
            B2f[:, f * NT:(f + 1) * NT], B2f[:, f * NT:(f + 1) * NT],
            cv("gbc", f), B1f[:, f * NT:(f + 1) * NT],
            op0=ALU.mult, op1=ALU.add),
            waits=[(sAD, v_x1re), (sDV, v_racc)], inc=sDV)
    sync_dma(lambda s: s.dma_start(
        dt["out"][:, :].rearrange("(f p) t -> p f t", p=128),
        B2f[:, 0:FC * NT].rearrange("p (f t) -> p f t", t=NT)),
        waits=[(sDV, v_fin)])

    with ctx:
        handles = {}
        for s in S.sems:
            handles[s.name] = ctx.enter_context(nc.semaphore(s.name))
        with nc.Block() as block:
            own_sem = {"vector": sDV, "scalar": sAC, "gpsimd": sGP}

            def make_section(eng_name):
                own = own_sem.get(eng_name)

                def run(eng):
                    own_cnt = 0
                    for waits, fn, inc, inc_n in S.steps[eng_name]:
                        if own is not None and own_cnt > 0:
                            eng.wait_ge(handles[own.name], own_cnt)
                        for sm, val in waits:
                            eng.wait_ge(handles[sm.name], val)
                        inst = fn(eng)
                        if inc is not None:
                            inst.then_inc(handles[inc.name], inc_n)
                            if inc is own:
                                own_cnt += inc_n
                return run
            block.tensor(make_section("tensor"))
            block.vector(make_section("vector"))
            block.scalar(make_section("scalar"))
            block.sync(make_section("sync"))
            block.gpsimd(make_section("gpsimd"))
    return nc


def _prep(inputs):
    f32 = np.float32
    x = np.asarray(inputs["L0_attn_x"], f32)
    vecs = np.stack([
        np.asarray(inputs["L0_attn_tm_mk"], f32).reshape(H),
        np.asarray(inputs["L0_attn_tm_mv"], f32).reshape(H),
        np.asarray(inputs["L0_attn_tm_mr"], f32).reshape(H),
        np.asarray(inputs["L0_attn_cm_mk"], f32).reshape(H),
        np.asarray(inputs["L0_attn_cm_mr"], f32).reshape(H),
        np.asarray(inputs["L0_attn_ln1_g"], f32).reshape(H),
        np.asarray(inputs["L0_attn_ln1_b"], f32).reshape(H),
        np.asarray(inputs["L0_attn_ln2_g"], f32).reshape(H),
        np.asarray(inputs["L0_attn_ln2_b"], f32).reshape(H),
        np.asarray(inputs["L0_attn_tm_first"], f32).reshape(H),
    ])
    shared = {
        "vecs": np.ascontiguousarray(vecs),
        "wmod": _tile_kxm(np.asarray(inputs["L0_wmod"], f32)),
        "wr": _tile_kxm(np.asarray(inputs["L0_attn_Wr"], f32).T),
        "wk": _tile_kxm(np.asarray(inputs["L0_attn_Wk"], f32).T),
        "wv": _tile_kxm(np.asarray(inputs["L0_attn_Wv"], f32).T),
        "wo": _tile_kxm(np.asarray(inputs["L0_attn_Wo"], f32).T),
        "wcr": _tile_kxm(np.asarray(inputs["L0_attn_Wcr"], f32).T),
        "wkey": _tile_kxm(np.asarray(inputs["L0_attn_Wkey"], f32).T),
        "wval": _tile_kxm(np.asarray(inputs["L0_attn_Wval"], f32).T),
        "wrouter": np.ascontiguousarray(
            np.asarray(inputs["L0_ffn_W_router"], f32)
            .reshape(16, 128, 16).transpose(1, 0, 2).reshape(128, 256)),
        "wshu": np.concatenate([_tile_kxm(np.asarray(inputs["L0_ffn_Wsh_up"], f32)[e])
                                for e in range(E_SH)]),
        "wshd": np.concatenate([_tile_kxm(np.asarray(inputs["L0_ffn_Wsh_down"], f32)[e])
                                for e in range(E_SH)]),
        "wrtu": np.concatenate([_tile_kxm(np.asarray(inputs["L0_ffn_Wrt_up"], f32)[e])
                                for e in range(E_RT)]),
        "wrtd": np.concatenate([_tile_kxm(np.asarray(inputs["L0_ffn_Wrt_down"], f32)[e])
                                for e in range(E_RT)]),
    }
    c = np.asarray(inputs["L0_c"], f32)
    in_maps = []
    for core in range(8):
        b, s = core // 2, core % 2
        t0 = s * NT
        xc = np.zeros((W, H), f32)
        xc[2:] = x[b, t0:t0 + NT]
        if s == 1:
            xc[0:2] = x[b, t0 - 2:t0]
        m = dict(shared)
        m["x0"] = np.ascontiguousarray(xc.T)
        m["crow"] = np.ascontiguousarray(c[b].reshape(16, 128))
        m["hmask"] = np.full((128, 1), float(s), f32)
        in_maps.append(m)
    return in_maps


def kernel(**inputs):
    in_maps = _prep(inputs)
    nc = build_nc()
    res = run_bass_kernel_spmd(nc, in_maps, core_ids=list(range(8)))
    y = np.empty((B, T, H), np.float32)
    for core in range(8):
        b, s = core // 2, core % 2
        y[b, s * NT:(s + 1) * NT, :] = res.results[core]["out"].T
    return y
